# version 02
# baseline (speedup 1.0000x reference)
import numpy as np
import concourse.bass as bass
import concourse.mybir as mybir

F32 = mybir.dt.float32
BF16 = mybir.dt.bfloat16
I32 = mybir.dt.int32
AF = mybir.ActivationFunctionType
ALU = mybir.AluOpType
X = mybir.AxisListType.X
MAGIC = 1.5 * 2.0**23


def full_cfg():
    return dict(D=1024, DI=2048, DS=16, DFF=4096, NB=2, T=4096, W=256,
                debug=False)


def small_cfg():
    return dict(D=256, DI=512, DS=4, DFF=512, NB=2, T=128, W=64, debug=True)


class Pending:
    __slots__ = ("val",)

    def __init__(self):
        self.val = None


NSLOT = 8


class Track:

    DMA_Q = ("QS", "QG")

    def __init__(self):
        engs = ["PE", "ACT", "DVE"]
        self.sems = list(engs)
        for q in self.DMA_Q:
            self.sems += [f"{q}{i}" for i in range(NSLOT)]
        self.cnt = {k: 0 for k in self.sems}
        self.rr = {q: 0 for q in self.DMA_Q}
        self.last_w = {}
        self.readers = {}
        self.op_seq = {k: 0 for k in ["PE", "ACT", "DVE", "QS", "QG"]}
        self.drain_seq = {k: -1 for k in ["PE", "ACT", "DVE", "QS", "QG"]}
        self.pending = {k: [] for k in ["PE", "ACT", "DVE"]}
        self.records = []
        self.ev_idx = {}
        self.pending_idx = {}

    def _resolve(self, eng):
        for p in self.pending[eng]:
            p.val = self.cnt[eng]
        self.pending[eng] = []

    def emit(self, eng, emitfn, reads=(), writes=(), inc=True):
        pre = []
        need_drain = False
        is_q = eng in self.DMA_Q

        def wait_on(sem, pc, peng, pseq):
            nonlocal need_drain
            if peng == eng and not is_q:
                if pseq > self.drain_seq[eng]:
                    need_drain = True
                return
            if peng == eng and is_q:
                pre.append(("w", sem, pc))
                return
            if isinstance(pc, Pending) and pc.val is None:
                self.cnt[sem] += 1
                self._resolve(sem)
                self.ev_idx[(sem, self.cnt[sem])] = len(self.records)
                self.records.append((sem, [("d",)], "BUMP", sem, 1))
                self.op_seq[sem] += 1
                self.drain_seq[sem] = self.op_seq[sem] - 1
                pc = self.cnt[sem]
            pre.append(("w", sem, pc))

        for r in reads:
            lw = self.last_w.get(r)
            if lw is not None:
                wait_on(*lw)
        for w in writes:
            lw = self.last_w.get(w)
            if lw is not None:
                wait_on(*lw)
            for sem, (pc, peng, pseq) in self.readers.get(w, {}).items():
                wait_on(sem, pc, peng, pseq)
        if need_drain:
            pre.insert(0, ("d",))
            self.drain_seq[eng] = self.op_seq[eng] - 1
        myseq = self.op_seq[eng]
        self.op_seq[eng] += 1
        if is_q:
            slot = f"{eng}{self.rr[eng]}"
            self.rr[eng] = (self.rr[eng] + 1) % NSLOT
            if self.cnt[slot] > 0:
                pre.append(("w", slot, self.cnt[slot]))
            self.cnt[slot] += 16
            c = self.cnt[slot]
            sem_tag, step = slot, 16
        elif inc:
            self.cnt[eng] += 1
            self._resolve(eng)
            c = self.cnt[eng]
            sem_tag, step = eng, 1
        else:
            c = Pending()
            self.pending[eng].append(c)
            sem_tag, step = eng, 0
        for w in writes:
            self.last_w[w] = (sem_tag, c, eng, myseq)
            self.readers[w] = {}
        for r in reads:
            self.readers.setdefault(r, {})[
                eng if not is_q else sem_tag] = (c, eng, myseq)
        if step:
            self.ev_idx[(sem_tag, c)] = len(self.records)
        else:
            self.pending_idx.setdefault(eng, []).append(
                (c, len(self.records)))
        self.records.append((eng, pre, emitfn, sem_tag, step))

    def verify_order(self):
        bad = 0
        for idx, (eng, pre, _, _, _) in enumerate(self.records):
            for a in pre:
                if a[0] != "w":
                    continue
                _, sm, pc = a
                val = pc.val if isinstance(pc, Pending) else pc
                pidx = self.ev_idx.get((sm, val))
                if pidx is None or pidx > idx:
                    print(f"FORWARD EDGE: record {idx} ({eng}) waits "
                          f"{sm}>={val} produced at record {pidx}")
                    bad += 1
                    if bad > 20:
                        return


def ceil_div(a, b):
    return (a + b - 1) // b


def build_program(cfg):
    D, DI, DS, DFF = cfg["D"], cfg["DI"], cfg["DS"], cfg["DFF"]
    NB, T, W = cfg["NB"], cfg["T"], cfg["W"]
    debug = cfg.get("debug", False)
    COLS = NB * W
    NPASS = T // W
    PD, PDI, PFF = D // 128, DI // 128, DFF // 128
    NTS = ceil_div(W, 128)
    assert COLS <= 512

    nc = bass.Bass("TRN2", target_bir_lowering=False)

    x_in = nc.dram_tensor("x_in", [NB, T, D], F32, kind="ExternalInput")
    W_in = nc.dram_tensor("W_in", [D, 2 * DI], F32, kind="ExternalInput")
    b_in = nc.dram_tensor("b_in", [2 * DI], F32, kind="ExternalInput")
    W_xp = nc.dram_tensor("W_xp", [DI, 2 * DS], F32, kind="ExternalInput")
    b_xp = nc.dram_tensor("b_xp", [2 * DS], F32, kind="ExternalInput")
    W_dt = nc.dram_tensor("W_dt", [DI, DI], F32, kind="ExternalInput")
    b_dt = nc.dram_tensor("b_dt", [DI], F32, kind="ExternalInput")
    W_out = nc.dram_tensor("W_out", [DI, D], F32, kind="ExternalInput")
    b_out = nc.dram_tensor("b_out", [D], F32, kind="ExternalInput")
    A_log = nc.dram_tensor("A_log", [DI, DS], F32, kind="ExternalInput")
    D_par = nc.dram_tensor("D_par", [DI], F32, kind="ExternalInput")
    w_g = nc.dram_tensor("w_g", [DFF, D], F32, kind="ExternalInput")
    w_u = nc.dram_tensor("w_u", [DFF, D], F32, kind="ExternalInput")
    w_d = nc.dram_tensor("w_d", [D, DFF], F32, kind="ExternalInput")
    ident = nc.dram_tensor("ident", [128, 128], F32, kind="ExternalInput")
    identb = nc.dram_tensor("identb", [128, 128], BF16, kind="ExternalInput")
    x_out = nc.dram_tensor("x_out", [NB, T, D], F32, kind="ExternalOutput")

    def scratch(name, K, M):
        return nc.dram_tensor(name, [K // 128, 128, M], BF16, kind="Internal")

    s_Win = scratch("s_Win", D, 2 * DI)
    s_Wxp = scratch("s_Wxp", DI, 2 * DS)
    s_Wdt = scratch("s_Wdt", DI, DI)
    s_Wout = scratch("s_Wout", DI, D)
    s_wg = scratch("s_wg", D, DFF)
    s_wu = scratch("s_wu", D, DFF)
    s_wd = scratch("s_wd", DFF, D)

    dbg = {}
    if debug:
        for name, shape in [
            ("d_u", [128, PDI * COLS]), ("d_dt", [128, PDI * COLS]),
            ("d_sz", [128, PDI * COLS]),
            ("d_yg", [128, PDI * COLS]), ("d_x1", [128, PD * COLS]),
            ("d_xqs", [128, PD * COLS]), ("d_v", [128, PFF * W]),
            ("d_ffo", [128, PD * COLS]), ("d_xn", [128, PD * COLS]),
        ]:
            dbg[name] = nc.dram_tensor(name, shape, F32, kind="ExternalOutput")

    from contextlib import ExitStack
    es = ExitStack()
    _pitch = {}

    def S(name, shape, dt):
        h = es.enter_context(nc.sbuf_tensor(name, shape, dt))
        _pitch[id(h)] = shape[1]
        return h

    with es:
        x_res = S("x_res", [128, PD * COLS], F32)
        u = S("u", [128, PDI * COLS], BF16)
        sz = S("sz", [128, PDI * COLS], BF16)
        dt_t = S("dt_t", [128, PDI * COLS], BF16)
        xqs = S("xqs", [128, PD * COLS], BF16)
        vbuf = S("vbuf", [128, PFF * W], BF16)
        stage = S("stage", [128, D], F32)
        scr16a = S("scr16a", [128, PD * COLS], BF16)
        scr16b = S("scr16b", [128, PD * COLS], BF16)
        SH = max(1, DS // 2)
        dA = S("dA", [128, SH * COLS], BF16)
        dBu = S("dBu", [128, 2 * W], BF16)
        Harena = S("Harena", [128, SH * W], BF16)
        Parena = S("Parena", [128, SH * W], BF16)
        Ptmp = S("Ptmp", [128, max(1, SH // 2) * W], BF16)
        ypart = S("ypart", [128, COLS], BF16)
        wtile = S("wtile", [128, COLS], BF16)
        ytmp = S("ytmp", [128, W], BF16)
        Bfull = S("Bfull", [128, DS * COLS], BF16)
        Cfull = S("Cfull", [128, DS * COLS], BF16)
        colmax = S("colmax", [128, COLS], BF16)
        colmin = S("colmin", [128, COLS], BF16)
        carry = S("carry", [128, NB * PDI * DS], F32)
        e_ring = S("e_ring", [128, COLS], F32)
        sq_ring = S("sq_ring", [128, 2 * COLS], BF16)
        g_ring = S("g_ring", [128, 2 * W], BF16)
        vq_ring = S("vq_ring", [128, 2 * W], BF16)
        i32_ring = S("i32_ring", [128, COLS], I32)
        rows = S("rows", [1, 7 * COLS], F32)
        rows16 = S("rows16", [1, 2 * COLS], BF16)
        bct_a = S("bct_a", [128, COLS], BF16)
        bct_b = S("bct_b", [128, COLS], BF16)
        bct_c = S("bct_c", [128, COLS], F32)
        bct_d = S("bct_d", [128, COLS], F32)
        wp = S("wp", [128, 2 * 1024], BF16)
        idn = S("idn", [128, 128], F32)
        idnb = S("idnb", [128, 128], BF16)
        Acols = S("Acols", [128, PDI * DS], F32)
        bcol_in = S("bcol_in", [128, 2 * PDI], F32)
        bcol_dt = S("bcol_dt", [128, PDI], F32)
        bcol_out = S("bcol_out", [128, PD], F32)
        bcol_xp = S("bcol_xp", [2 * DS, 1], F32)
        Dcols = S("Dcols", [128, PDI], F32)
        mw_col = S("mw_col", [128, 1], F32)
        ones16 = S("ones16", [128, 1], BF16)
        ones32 = S("ones32", [128, 1], F32)
        c_eps = S("c_eps", [128, 1], F32)
        c_one = S("c_one", [128, 1], F32)
        scal = S("scal", [1, 16], F32)
        qsum = S("qsum", [128, 64], F32)
        if debug:
            dstage = S("dstage", [128, max(PDI, PFF) * COLS], F32)
        psum = es.enter_context(nc.psum_tensor("psum", [128, 3584], F32))
        psumB = es.enter_context(nc.psum_tensor("psumB", [128, 1024], BF16))

        tr = Track()

        def psA(n, p0=0, pn=128):
            return psum[p0:p0 + pn, 0:n]

        def psB(n):
            return psum[:, 512:512 + n]

        def psC(n, pn=128):
            return psum[0:pn, 1024:1024 + n]

        def psT(p0, pn, n):
            return psum[p0:p0 + pn, 1024:1024 + n]

        def psTB(p0, pn, n):
            return psumB[p0:p0 + pn, 0:n]

        def psS1(n):
            return psum[0:1, 1024:1024 + n]

        def psS2(n):
            return psum[0:1, 1536:1536 + n]

        def psDN(mc, n):
            return psum[:, 1536 + mc * n:1536 + (mc + 1) * n]

        def dma(q, out_ap, in_ap, reads, writes, inc=True):
            tr.emit(q, lambda e: e.dma_start(
                out=out_ap, in_=in_ap, allow_slow_non_contiguous=True),
                reads, writes, inc)

        def act(out_ap, in_ap, func, reads, writes, bias=0.0, scale=1.0,
                inc=True):
            tr.emit("ACT",
                    lambda e: e.activation(out_ap, in_ap, func, bias=bias,
                                           scale=scale),
                    reads, writes, inc)

        def dve(emitfn, reads, writes, inc=True):
            tr.emit("DVE", emitfn, reads, writes, inc)

        def pe(emitfn, reads, writes, inc=True):
            tr.emit("PE", emitfn, reads, writes, inc)

        def AP(t, off, ap):
            p = _pitch.get(id(t))
            if p is not None:
                ap = [[p, ap[0][1]]] + [list(x) for x in ap[1:]]
            return bass.AP(t, off, ap)

        dma("QS", idn[:], ident.ap(), [], ["idn"])
        dma("QS", idnb[:], identb.ap(), [], ["idnb"])
        dma("QS", bcol_in[:], AP(b_in, 0, [[1, 128], [128, 2 * PDI]]),
            [], ["bcol_in"])
        dma("QS", bcol_dt[:], AP(b_dt, 0, [[1, 128], [128, PDI]]),
            [], ["bcol_dt"])
        dma("QS", bcol_out[:], AP(b_out, 0, [[1, 128], [128, PD]]),
            [], ["bcol_out"])
        dma("QS", bcol_xp[:], AP(b_xp, 0, [[1, 2 * DS], [1, 1]]),
            [], ["bcol_xp"])
        dma("QS", Dcols[:], AP(D_par, 0, [[1, 128], [128, PDI]]),
            [], ["Dcols"])
        dma("QS", Acols[:], AP(A_log, 0, [[DS, 128], [128 * DS, PDI], [1, DS]]),
            [], ["Acols"])
        act(Acols[:], Acols[:], AF.Exp, ["Acols"], ["Acols"])
        dve(lambda e: e.tensor_scalar(Acols[:], Acols[:], -1.0, None,
                                      ALU.mult), ["Acols"], ["Acols"])
        dve(lambda e: e.memset(carry[:], 0.0), [], ["carry"])
        dve(lambda e: e.memset(ones16[:], 1.0), [], ["ones16"])
        dve(lambda e: e.memset(ones32[:], 1.0), [], ["ones32"])
        dve(lambda e: e.memset(c_eps[:], 1e-5), [], ["c_eps"])
        dve(lambda e: e.memset(c_one[:], 1.0), [], ["c_one"])

        CHK = 2048 if PD * COLS >= 4096 else (PD * COLS) // 2

        def wconv_ap(n):
            return AP(x_res, 0, [[1, 128], [1, n]])

        def qtmp_ap(n):
            return AP(x_res, CHK, [[1, 128], [1, n]])

        def prepass_cast(src, dst, K, M):
            for kc in range(K // 128):
                for c0 in range(0, M, CHK):
                    cw = min(CHK, M - c0)
                    dma("QS", wconv_ap(cw),
                        src.ap()[kc * 128:(kc + 1) * 128, c0:c0 + cw],
                        ["x_res"], ["x_res"])
                    act(AP(scr16a, 0, [[1, 128], [1, cw]]), wconv_ap(cw),
                        AF.Copy, ["x_res"], ["scr16a"])
                    dma("QS", dst.ap()[kc, :, c0:c0 + cw],
                        AP(scr16a, 0, [[1, 128], [1, cw]]),
                        ["scr16a"], [dst.name + "_hbm"])

        def prepass_quant(src, dst, R, Cc, scal_idx):
            nblk = R // 128
            nch = ceil_div(Cc, CHK)
            for kc in range(nblk):
                for ci in range(nch):
                    c0 = ci * CHK
                    cw = min(CHK, Cc - c0)
                    qi = kc * nch + ci
                    dma("QS", wconv_ap(cw),
                        src.ap()[kc * 128:(kc + 1) * 128, c0:c0 + cw],
                        ["x_res"], ["x_res"])
                    act(qtmp_ap(cw), wconv_ap(cw), AF.Abs, ["x_res"], ["x_res"])
                    dve(lambda e, qi=qi, cw=cw: e.tensor_reduce(
                        qsum[:, qi:qi + 1], qtmp_ap(cw), X, ALU.add),
                        ["x_res"], ["qsum"])
            nq = nblk * nch
            pe(lambda e, nq=nq: e.matmul(psS1(nq), ones32[:], qsum[:, 0:nq],
                                         start=True, stop=True),
               ["ones32", "qsum"], ["ps_c"])
            dve(lambda e: e.tensor_reduce(scal[0:1, 0:1], psS1(nq), X,
                                          ALU.add), ["ps_c"], ["scal"])
            dve(lambda e: e.tensor_scalar(
                scal[0:1, 0:1], scal[0:1, 0:1], 1.0 / (R * Cc), 1e-5,
                ALU.mult, ALU.max), ["scal"], ["scal"])
            if scal_idx == 0:
                dve(lambda e: e.tensor_copy(scal[0:1, 2:3], scal[0:1, 0:1]),
                    ["scal"], ["scal"])
            act(scal[0:1, 1:2], scal[0:1, 0:1], AF.Ln, ["scal"], ["scal"])
            act(scal[0:1, 1:2], scal[0:1, 1:2], AF.Exp, ["scal"], ["scal"],
                scale=-1.0)
            dma("QG", mw_col[:], AP(scal, 1, [[1, 1], [0, 128], [1, 1]]),
                ["scal"], ["mw_col"])
            for kc in range(nblk):
                for ci in range(nch):
                    c0 = ci * CHK
                    cw = min(CHK, Cc - c0)
                    dma("QS", wconv_ap(cw),
                        src.ap()[kc * 128:(kc + 1) * 128, c0:c0 + cw],
                        ["x_res", dst.name + "_hbm"], ["x_res"])
                    dve(lambda e, cw=cw: e.tensor_scalar(
                        qtmp_ap(cw), wconv_ap(cw), mw_col[:, 0:1], None,
                        ALU.mult), ["x_res", "mw_col"], ["x_res"])
                    dve(lambda e, cw=cw: e.tensor_scalar(
                        qtmp_ap(cw), qtmp_ap(cw), MAGIC, MAGIC, ALU.add,
                        ALU.subtract), ["x_res"], ["x_res"])
                    dve(lambda e, cw=cw: e.tensor_scalar(
                        qtmp_ap(cw), qtmp_ap(cw), 1.0, -1.0, ALU.min,
                        ALU.max), ["x_res"], ["x_res"])
                    for f0 in range(0, cw, COLS):
                        fw = min(COLS, cw - f0)
                        dve(lambda e, f0=f0, fw=fw: e.tensor_copy(
                            AP(i32_ring, 0, [[1, 128], [1, fw]]),
                            AP(x_res, CHK + f0, [[1, 128], [1, fw]])),
                            ["x_res"], ["i32_ring"])
                        dve(lambda e, f0=f0, fw=fw: e.tensor_copy(
                            AP(scr16a, f0, [[1, 128], [1, fw]]),
                            AP(i32_ring, 0, [[1, 128], [1, fw]])),
                            ["i32_ring"], ["scr16a"])
                    for t in range(cw // 128):
                        ck = (c0 + t * 128) // 128
                        pe(lambda e, t=t: e.transpose(
                            psTB(0, 128, 128),
                            AP(scr16a, t * 128, [[1, 128], [1, 128]]),
                            idnb[:]), ["scr16a", "idnb"], ["ps_tb"])
                        act(AP(scr16b, (t % 4) * 128, [[1, 128], [1, 128]]),
                            psTB(0, 128, 128), AF.Copy, ["ps_tb"], ["scr16b"])
                        dma("QS", dst.ap()[ck, :, kc * 128:(kc + 1) * 128],
                            AP(scr16b, (t % 4) * 128, [[1, 128], [1, 128]]),
                            ["scr16b"], [dst.name + "_hbm"])

        prepass_cast(W_in, s_Win, D, 2 * DI)
        prepass_cast(W_xp, s_Wxp, DI, 2 * DS)
        prepass_cast(W_dt, s_Wdt, DI, DI)
        prepass_cast(W_out, s_Wout, DI, D)
        prepass_quant(w_g, s_wg, DFF, D, 0)
        prepass_quant(w_u, s_wu, DFF, D, 1)
        prepass_quant(w_d, s_wd, D, DFF, 2)
        dma("QG", mw_col[:], AP(scal, 2, [[1, 1], [0, 128], [1, 1]]),
            ["scal"], ["mw_col"])

        def load_piece(half, src, kcs, mc0, nmc, tag):
            hb = "wp%d" % half
            assert len(kcs) * nmc * 128 <= 1024
            for i, kc in enumerate(kcs):
                dma("QS",
                    AP(wp, half * 1024 + i * nmc * 128,
                       [[1, 128], [1, nmc * 128]]),
                    src.ap()[kc, :, mc0 * 128:(mc0 + nmc) * 128],
                    [hb, tag], [hb])

        def piece_tile(half, i_kc, j, nmc):
            return AP(wp, half * 1024 + (i_kc * nmc + j) * 128,
                      [[1, 128], [1, 128]])

        RO = dict(S1=0, MU=1, VAR=2, R=3, NMR=4, M1=5, TMP=6)
        RO["TMP2"] = RO["VAR"]
        RO["S1INV"] = RO["MU"]

        def row(name, n=None):
            o = RO[name] * COLS
            return rows[0:1, o:o + (n or COLS)]

        def stats(srcname, src_ap_fn, nblk):
            for i in range(nblk):
                pe(lambda e, i=i: e.matmul(psS1(COLS), ones16[:],
                                           src_ap_fn(i), start=(i == 0),
                                           stop=(i == nblk - 1)),
                   [srcname, "ones16"], ["ps_c"], inc=(i == nblk - 1))
            for i in range(nblk):
                sqs = (i % 2) * COLS
                act(AP(sq_ring, sqs, [[1, 128], [1, COLS]]), src_ap_fn(i),
                    AF.Square, [srcname], ["sq_ring"])
                pe(lambda e, i=i, sqs=sqs: e.matmul(
                    psS2(COLS), ones16[:],
                    AP(sq_ring, sqs, [[1, 128], [1, COLS]]),
                    start=(i == 0), stop=(i == nblk - 1)),
                   ["sq_ring", "ones16"], ["ps_dn"], inc=(i == nblk - 1))

        def ln_rows(Dn):
            dve(lambda e: e.tensor_scalar(row("MU"), psS1(COLS), 1.0 / Dn,
                                          None, ALU.mult), ["ps_c"], ["rows"])
            dve(lambda e: e.tensor_scalar(row("VAR"), psS2(COLS), 1.0 / Dn,
                                          None, ALU.mult), ["ps_dn"], ["rows"])
            dve(lambda e: e.scalar_tensor_tensor(
                row("TMP"), row("MU"), 1.0, row("MU"), ALU.bypass, ALU.mult),
                ["rows"], ["rows"])
            dve(lambda e: e.tensor_tensor(row("VAR"), row("VAR"), row("TMP"),
                                          ALU.subtract), ["rows"], ["rows"])
            act(row("R"), row("VAR"), AF.Ln, ["rows", "c_eps"], ["rows"],
                bias=c_eps[0:1, 0:1])
            act(row("R"), row("R"), AF.Exp, ["rows"], ["rows"], scale=-0.5)
            dve(lambda e: e.scalar_tensor_tensor(
                row("NMR"), row("MU"), -1.0, row("R"), ALU.mult, ALU.mult),
                ["rows"], ["rows"])

        def bcast16(rsrc_ap, slot16, dst_ap, dst_name):
            dve(lambda e: e.tensor_copy(
                rows16[0:1, slot16 * COLS:slot16 * COLS + COLS], rsrc_ap),
                ["rows"], ["rows16"])
            dma("QG", dst_ap,
                AP(rows16, slot16 * COLS, [[1, 1], [0, 128], [1, COLS]]),
                ["rows16"], [dst_name])

        def bcast32(roff, n, dst_ap, dst_name):
            dma("QG", dst_ap, AP(rows, roff, [[1, 1], [0, 128], [1, n]]),
                ["rows"], [dst_name])

        def ln_apply16(srcname, src_ap_fn, dstname, dst_ap_fn, nblk):
            bcast16(row("R"), 0, bct_a[:], "bct_a")
            bcast16(row("NMR"), 1, bct_b[:], "bct_b")
            for i in range(nblk):
                dve(lambda e, i=i: e.tensor_tensor(
                    dst_ap_fn(i), src_ap_fn(i), bct_a[:], ALU.mult),
                    [srcname, "bct_a"], [dstname])
                dve(lambda e, i=i: e.tensor_tensor(
                    dst_ap_fn(i), dst_ap_fn(i), bct_b[:], ALU.add),
                    [dstname, "bct_b"], [dstname])

        def dbg_dump(key, src_ap, n, readname):
            if not debug:
                return
            act(AP(dstage, 0, [[1, 128], [1, n]]) if n > 0 else None, src_ap,
                AF.Copy, [readname], ["dstage"])
            dma("QG", dbg[key].ap(), AP(dstage, 0, [[1, 128], [1, n]]),
                ["dstage"], [key])

        for p in range(NPASS):
            t0 = p * W
            for b in range(NB):
                for ts in range(NTS):
                    rws = min(128, W - ts * 128)
                    dma("QS", AP(stage, 0, [[1, rws], [1, D]]),
                        x_in.ap()[b, t0 + ts * 128:t0 + ts * 128 + rws, :],
                        ["stage"], ["stage"])
                    for dpb in range(PD):
                        pe(lambda e, ts=ts, dpb=dpb, rws=rws: e.transpose(
                            psT(0, 128, rws),
                            AP(stage, dpb * 128, [[1, rws], [1, 128]]),
                            idn[0:rws, 0:rws]),
                           ["stage", "idn"], ["ps_c"])
                        act(AP(x_res, dpb * COLS + b * W + ts * 128,
                               [[1, 128], [1, rws]]),
                            psT(0, 128, rws), AF.Copy, ["ps_c"], ["x_res"])
            act(scr16b[:], x_res[:], AF.Copy, ["x_res"], ["scr16b"])

            def b16(i):
                return AP(scr16b, i * COLS, [[1, 128], [1, COLS]])

            def a16(i):
                return AP(scr16a, i * COLS, [[1, 128], [1, COLS]])

            xn16 = b16

            stats("scr16b", b16, PD)
            ln_rows(D)
            ln_apply16("scr16b", b16, "scr16b", xn16, PD)
            dbg_dump("d_xn", scr16b[:], PD * COLS, "scr16b")

            half = 0
            NMC_IN = 2 * PDI
            for mc in range(NMC_IN):
                load_piece(half, s_Win, list(range(PD)), mc, 1, "s_Win_hbm")
                for kc in range(PD):
                    pe(lambda e, kc=kc, half=half:
                       e.matmul(psA(COLS), piece_tile(half, kc, 0, 1),
                                xn16(kc), start=(kc == 0),
                                stop=(kc == PD - 1)),
                       ["wp%d" % half, "scr16b"], ["ps_a"],
                       inc=(kc == PD - 1))
                dst, dsn, mcl = (u, "u", mc) if mc < PDI else \
                    (sz, "sz", mc - PDI)
                act(AP(dst, mcl * COLS, [[1, 128], [1, COLS]]),
                    psA(COLS), AF.Silu, ["ps_a", "bcol_in"], [dsn],
                    bias=bcol_in[:, mc:mc + 1])
                half ^= 1
            dbg_dump("d_u", u[:], PDI * COLS, "u")
            dbg_dump("d_sz", sz[:], PDI * COLS, "sz")

            def usl(i):
                return AP(u, i * COLS, [[1, 128], [1, COLS]])

            NKP = max(1, 2048 // (2 * DS))
            for pc in range(ceil_div(PDI, NKP)):
                kcs = list(range(pc * NKP, min((pc + 1) * NKP, PDI)))
                hb = "wp%d" % half
                for i, kc in enumerate(kcs):
                    dma("QS", AP(wp, half * 1024 + i * 2 * DS,
                                 [[1, 128], [1, 2 * DS]]),
                        s_Wxp.ap()[kc, :, 0:2 * DS],
                        [hb, "s_Wxp_hbm"], [hb])
                for i, kc in enumerate(kcs):
                    pe(lambda e, i=i, kc=kc, half=half: e.matmul(
                        psC(COLS, 2 * DS),
                        AP(wp, half * 1024 + i * 2 * DS,
                           [[1, 128], [1, 2 * DS]]),
                        usl(kc), start=(kc == 0), stop=(kc == PDI - 1)),
                       [hb, "u"], ["ps_c"], inc=(kc == PDI - 1))
                half ^= 1
            act(AP(sq_ring, 0, [[1, 2 * DS], [1, COLS]]), psC(COLS, 2 * DS),
                AF.Identity, ["ps_c", "bcol_xp"], ["sq_ring"],
                bias=bcol_xp[:, 0:1])

            KHALF = max(1, PDI // 2)
            for mc in range(PDI):
                for kh in range(ceil_div(PDI, KHALF)):
                    kcs = list(range(kh * KHALF, min((kh + 1) * KHALF, PDI)))
                    load_piece(half, s_Wdt, kcs, mc, 1, "s_Wdt_hbm")
                    for i, kc in enumerate(kcs):
                        pe(lambda e, i=i, kc=kc, half=half: e.matmul(
                            psA(COLS), piece_tile(half, i, 0, 1), usl(kc),
                            start=(kc == 0), stop=(kc == PDI - 1)),
                           ["wp%d" % half, "u"], ["ps_a"],
                           inc=(kc == PDI - 1))
                    half ^= 1
                er = 0
                act(AP(e_ring, er, [[1, 128], [1, COLS]]), psA(COLS), AF.Exp,
                    ["ps_a", "bcol_dt"], ["e_ring"], bias=bcol_dt[:, mc:mc + 1])
                act(AP(dt_t, mc * COLS, [[1, 128], [1, COLS]]),
                    AP(e_ring, er, [[1, 128], [1, COLS]]), AF.Ln,
                    ["e_ring", "c_one"], ["dt_t"], bias=c_one[:, 0:1])
            dbg_dump("d_dt", dt_t[:], PDI * COLS, "dt_t")

            sqp = 2 * COLS
            for s in range(DS):
                for b in range(NB):
                    dma("QG", AP(Bfull, s * COLS + b * W, [[1, 128], [1, W]]),
                        bass.AP(sq_ring, s * sqp + b * W,
                                [[sqp, 1], [0, 128], [1, W]]),
                        ["sq_ring"], ["Bfull"])
                    dma("QG", AP(Cfull, s * COLS + b * W, [[1, 128], [1, W]]),
                        bass.AP(sq_ring, (DS + s) * sqp + b * W,
                                [[sqp, 1], [0, 128], [1, W]]),
                        ["sq_ring"], ["Cfull"])
            NSH = ceil_div(DS, SH)
            for dblk in range(PDI):
                dve(lambda e, dblk=dblk: e.tensor_tensor(
                    wtile[:], AP(dt_t, dblk * COLS, [[1, 128], [1, COLS]]),
                    AP(u, dblk * COLS, [[1, 128], [1, COLS]]),
                    ALU.mult), ["dt_t", "u"], ["wtile"], inc=False)
                for sh in range(NSH):
                    s0 = sh * SH
                    ns = min(SH, DS - s0)
                    for sl in range(ns):
                        s = s0 + sl
                        act(AP(dA, sl * COLS, [[1, 128], [1, COLS]]),
                            AP(dt_t, dblk * COLS, [[1, 128], [1, COLS]]),
                            AF.Exp, ["dt_t", "Acols"], ["dA"],
                            scale=Acols[:, dblk * DS + s:dblk * DS + s + 1],
                            inc=(sl == ns - 1))
                    for b in range(NB):
                        for sl in range(ns):
                            s = s0 + sl
                            sl2 = (sl % 2) * W
                            dve(lambda e, s=s, b=b, sl2=sl2: e.tensor_tensor(
                                AP(dBu, sl2, [[1, 128], [1, W]]),
                                AP(wtile, b * W, [[1, 128], [1, W]]),
                                AP(Bfull, s * COLS + b * W,
                                   [[1, 128], [1, W]]),
                                ALU.mult),
                                ["wtile", "Bfull"], ["dBu"], inc=False)
                            ccol = b * PDI * DS + dblk * DS + s
                            dve(lambda e, sl=sl, b=b, ccol=ccol, sl2=sl2:
                                e.tensor_tensor_scan(
                                    AP(Harena, sl * W, [[1, 128], [1, W]]),
                                    AP(dA, sl * COLS + b * W,
                                       [[1, 128], [1, W]]),
                                    AP(dBu, sl2, [[1, 128], [1, W]]),
                                    carry[:, ccol:ccol + 1], ALU.mult,
                                    ALU.add),
                                ["dA", "dBu", "carry"], ["Harena"], inc=False)
                            dve(lambda e, sl=sl, s=s, b=b: e.tensor_tensor(
                                AP(Parena, sl * W, [[1, 128], [1, W]]),
                                AP(Harena, sl * W, [[1, 128], [1, W]]),
                                AP(Cfull, s * COLS + b * W,
                                   [[1, 128], [1, W]]),
                                ALU.mult),
                                ["Harena", "Cfull"], ["Parena"], inc=False)
                        cbase = b * PDI * DS + dblk * DS + s0
                        dve(lambda e, cbase=cbase, ns=ns: e.tensor_copy(
                            carry[:, cbase:cbase + ns],
                            AP(Harena, W - 1, [[1, 128], [W, ns]])),
                            ["Harena"], ["carry"], inc=False)
                        nsl = ns
                        srcbuf, srcname = Parena, "Parena"
                        while nsl > 1:
                            hn = nsl // 2
                            dstbuf, dstname = (Ptmp, "Ptmp") \
                                if srcbuf is Parena else (Parena, "Parena")
                            dve(lambda e, hn=hn, srcbuf=srcbuf,
                                dstbuf=dstbuf: e.tensor_tensor(
                                    AP(dstbuf, 0,
                                       [[1, 128], [W, hn], [1, W]]),
                                    AP(srcbuf, 0,
                                       [[1, 128], [2 * W, hn], [1, W]]),
                                    AP(srcbuf, W,
                                       [[1, 128], [2 * W, hn], [1, W]]),
                                    ALU.add), [srcname], [dstname],
                                inc=False)
                            srcbuf, srcname = dstbuf, dstname
                            nsl = hn
                        if sh == 0:
                            dve(lambda e, b=b, srcbuf=srcbuf: e.tensor_copy(
                                AP(ypart, b * W, [[1, 128], [1, W]]),
                                AP(srcbuf, 0, [[1, 128], [1, W]])),
                                [srcname], ["ypart"], inc=False)
                        else:
                            dve(lambda e, b=b, srcbuf=srcbuf:
                                e.tensor_tensor(
                                    AP(ypart, b * W, [[1, 128], [1, W]]),
                                    AP(ypart, b * W, [[1, 128], [1, W]]),
                                    AP(srcbuf, 0, [[1, 128], [1, W]]),
                                    ALU.add),
                                [srcname, "ypart"], ["ypart"], inc=False)
                for b in range(NB):
                    dve(lambda e, dblk=dblk, b=b: e.scalar_tensor_tensor(
                        ytmp[:], AP(u, dblk * COLS + b * W,
                                    [[1, 128], [1, W]]),
                        Dcols[:, dblk:dblk + 1],
                        AP(ypart, b * W, [[1, 128], [1, W]]),
                        ALU.mult, ALU.add),
                        ["ypart", "u", "Dcols"], ["ytmp"], inc=False)
                    dve(lambda e, dblk=dblk, b=b: e.tensor_tensor(
                        AP(u, dblk * COLS + b * W, [[1, 128], [1, W]]),
                        ytmp[:],
                        AP(sz, dblk * COLS + b * W, [[1, 128], [1, W]]),
                        ALU.mult), ["ytmp", "sz"], ["u"], inc=True)
            dbg_dump("d_yg", u[:], PDI * COLS, "u")

            for mc in range(PD):
                for kh in range(ceil_div(PDI, KHALF)):
                    kcs = list(range(kh * KHALF, min((kh + 1) * KHALF, PDI)))
                    load_piece(half, s_Wout, kcs, mc, 1, "s_Wout_hbm")
                    for i, kc in enumerate(kcs):
                        pe(lambda e, i=i, kc=kc, half=half: e.matmul(
                            psA(COLS), piece_tile(half, i, 0, 1), usl(kc),
                            start=(kc == 0), stop=(kc == PDI - 1)),
                           ["wp%d" % half, "u"], ["ps_a"],
                           inc=(kc == PDI - 1))
                    half ^= 1
                act(a16(mc), psA(COLS), AF.Identity, ["ps_a", "bcol_out"],
                    ["scr16a"], bias=bcol_out[:, mc:mc + 1])
            stats("scr16a", a16, PD)
            ln_rows(D)
            bcast16(row("R"), 0, bct_a[:], "bct_a")
            bcast16(row("NMR"), 1, bct_b[:], "bct_b")
            for i in range(PD):
                dve(lambda e, i=i: e.tensor_tensor(
                    b16(i), a16(i), bct_a[:], ALU.mult),
                    ["scr16a", "bct_a"], ["scr16b"])
                dve(lambda e, i=i: e.tensor_tensor(
                    b16(i), b16(i), bct_b[:], ALU.add),
                    ["scr16b", "bct_b"], ["scr16b"])
                dve(lambda e, i=i: e.tensor_tensor(
                    AP(x_res, i * COLS, [[1, 128], [1, COLS]]),
                    AP(x_res, i * COLS, [[1, 128], [1, COLS]]),
                    b16(i), ALU.add), ["x_res", "scr16b"], ["x_res"])
            dbg_dump("d_x1", x_res[:], PD * COLS, "x_res")

            act(scr16b[:], x_res[:], AF.Copy, ["x_res"], ["scr16b"])
            stats("scr16b", b16, PD)
            ln_rows(D)
            ln_apply16("scr16b", b16, "scr16b", xn16, PD)
            if PD > 1:
                dve(lambda e: e.tensor_tensor(colmax[:], xn16(0), xn16(1),
                                              ALU.max), ["scr16b"], ["colmax"])
                dve(lambda e: e.tensor_tensor(colmin[:], xn16(0), xn16(1),
                                              ALU.min), ["scr16b"], ["colmin"])
                for i in range(2, PD):
                    dve(lambda e, i=i: e.tensor_tensor(
                        colmax[:], colmax[:], xn16(i), ALU.max),
                        ["scr16b", "colmax"], ["colmax"])
                    dve(lambda e, i=i: e.tensor_tensor(
                        colmin[:], colmin[:], xn16(i), ALU.min),
                        ["scr16b", "colmin"], ["colmin"])
            else:
                dve(lambda e: e.tensor_copy(colmax[:], xn16(0)),
                    ["scr16b"], ["colmax"])
                dve(lambda e: e.tensor_copy(colmin[:], xn16(0)),
                    ["scr16b"], ["colmin"])
            dve(lambda e: e.tensor_scalar(colmin[:], colmin[:], -1.0, None,
                                          ALU.mult), ["colmin"], ["colmin"])
            dve(lambda e: e.tensor_tensor(colmax[:], colmax[:], colmin[:],
                                          ALU.max),
                ["colmax", "colmin"], ["colmax"])
            NCS = ceil_div(COLS, 128)
            for cs in range(NCS):
                cw = min(128, COLS - cs * 128)
                pe(lambda e, cs=cs, cw=cw: e.transpose(
                    psTB(0, cw, 128), colmax[:, cs * 128:cs * 128 + cw],
                    idnb[:]), ["colmax", "idnb"], ["ps_tb"])
                dve(lambda e, cs=cs, cw=cw: e.tensor_reduce(
                    qsum[0:cw, cs:cs + 1], psTB(0, cw, 128), X, ALU.max),
                    ["ps_tb"], ["qsum"])
            cw0 = min(128, COLS)
            for cs in range(NCS):
                dma("QG", AP(rows, RO["M1"] * COLS + cs * 128,
                             [[1, 1], [1, cw0]]),
                    qsum[0:cw0, cs:cs + 1], ["qsum"], ["rows"])
            dve(lambda e: e.tensor_scalar(row("TMP"), row("M1"), 1e-5, None,
                                          ALU.max), ["rows"], ["rows"])
            act(row("TMP2"), row("TMP"), AF.Ln, ["rows"], ["rows"])
            act(row("S1"), row("TMP2"), AF.Exp, ["rows"], ["rows"], scale=-1.0)
            dve(lambda e: e.tensor_scalar(row("S1"), row("S1"), 127.0, None,
                                          ALU.mult), ["rows"], ["rows"])
            dve(lambda e: e.tensor_scalar(row("S1INV"), row("TMP"),
                                          1.0 / 127.0, None, ALU.mult),
                ["rows"], ["rows"])
            bcast32(RO["S1"] * COLS, COLS, bct_c[:], "bct_c")
            bcast32(RO["S1INV"] * COLS, COLS, bct_d[:], "bct_d")
            for i in range(PD):
                ir = 0
                dve(lambda e, i=i, ir=ir: e.tensor_tensor(
                    AP(e_ring, ir, [[1, 128], [1, COLS]]), xn16(i), bct_c[:],
                    ALU.mult), ["scr16b", "bct_c"], ["e_ring"])
                dve(lambda e, ir=ir: e.tensor_scalar(
                    AP(i32_ring, ir, [[1, 128], [1, COLS]]),
                    AP(e_ring, ir, [[1, 128], [1, COLS]]), MAGIC, MAGIC,
                    ALU.add, ALU.subtract), ["e_ring"], ["i32_ring"])
                dve(lambda e, i=i, ir=ir: e.tensor_tensor(
                    AP(xqs, i * COLS, [[1, 128], [1, COLS]]),
                    AP(i32_ring, ir, [[1, 128], [1, COLS]]), bct_d[:],
                    ALU.mult), ["i32_ring", "bct_d"], ["xqs"])
            dbg_dump("d_xqs", xqs[:], PD * COLS, "xqs")

            for b in range(NB):
                dve(lambda e: e.memset(colmax[:, 0:W], -1e30), [], ["colmax"])
                dve(lambda e: e.memset(colmin[:, 0:W], 1e30), [], ["colmin"])
                NOC = max(1, 1024 // (PD * 128))
                for pc in range(ceil_div(PFF, NOC)):
                    ocs = list(range(pc * NOC, min((pc + 1) * NOC, PFF)))
                    load_piece(half, s_wg, list(range(PD)), ocs[0], len(ocs),
                               "s_wg_hbm")
                    h2 = half ^ 1
                    load_piece(h2, s_wu, list(range(PD)), ocs[0], len(ocs),
                               "s_wu_hbm")
                    for j, oc in enumerate(ocs):
                        for kc in range(PD):
                            pe(lambda e, kc=kc, j=j, half=half, n=len(ocs),
                               b=b: e.matmul(
                                   psA(W), piece_tile(half, kc, j, n),
                                   AP(xqs, kc * COLS + b * W,
                                      [[1, 128], [1, W]]),
                                   start=(kc == 0), stop=(kc == PD - 1)),
                               ["wp%d" % half, "xqs"], ["ps_a"],
                               inc=(kc == PD - 1))
                        gr = (oc % 2) * W
                        act(g_ring[:, gr:gr + W], psA(W), AF.Silu,
                            ["ps_a", "mw_col"], ["g_ring"],
                            scale=mw_col[:, 0:1])
                        for kc in range(PD):
                            pe(lambda e, kc=kc, j=j, h2=h2, n=len(ocs), b=b:
                               e.matmul(
                                   psB(W), piece_tile(h2, kc, j, n),
                                   AP(xqs, kc * COLS + b * W,
                                      [[1, 128], [1, W]]),
                                   start=(kc == 0), stop=(kc == PD - 1)),
                               ["wp%d" % h2, "xqs"], ["ps_b"],
                               inc=(kc == PD - 1))
                        dve(lambda e, oc=oc, gr=gr: e.tensor_tensor(
                            AP(vbuf, oc * W, [[1, 128], [1, W]]),
                            g_ring[:, gr:gr + W], psB(W), ALU.mult),
                            ["g_ring", "ps_b"], ["vbuf"])
                        dve(lambda e, oc=oc: e.tensor_tensor(
                            colmax[:, 0:W], colmax[:, 0:W],
                            AP(vbuf, oc * W, [[1, 128], [1, W]]), ALU.max),
                            ["vbuf", "colmax"], ["colmax"])
                        dve(lambda e, oc=oc: e.tensor_tensor(
                            colmin[:, 0:W], colmin[:, 0:W],
                            AP(vbuf, oc * W, [[1, 128], [1, W]]), ALU.min),
                            ["vbuf", "colmin"], ["colmin"])
                if b == 0:
                    dbg_dump("d_v", vbuf[:], PFF * W, "vbuf")
                dve(lambda e: e.tensor_scalar(colmin[:, 0:W], colmin[:, 0:W],
                                              -1.0, None, ALU.mult),
                    ["colmin"], ["colmin"])
                dve(lambda e: e.tensor_tensor(colmax[:, 0:W], colmax[:, 0:W],
                                              colmin[:, 0:W], ALU.max),
                    ["colmax", "colmin"], ["colmax"])
                NWS = ceil_div(W, 128)
                for cs in range(NWS):
                    cw = min(128, W - cs * 128)
                    pe(lambda e, cs=cs, cw=cw: e.transpose(
                        psTB(0, cw, 128), colmax[:, cs * 128:cs * 128 + cw],
                        idnb[:]), ["colmax", "idnb"], ["ps_tb"])
                    dve(lambda e, cs=cs, cw=cw: e.tensor_reduce(
                        qsum[0:cw, cs:cs + 1], psTB(0, cw, 128), X, ALU.max),
                        ["ps_tb"], ["qsum"])
                cww = min(128, W)
                for cs in range(NWS):
                    dma("QG", AP(rows, RO["M1"] * COLS + cs * 128,
                                 [[1, 1], [1, cww]]),
                        qsum[0:cww, cs:cs + 1], ["qsum"], ["rows"])
                dve(lambda e: e.tensor_scalar(row("TMP", W), row("M1", W),
                                              1e-5, None, ALU.max),
                    ["rows"], ["rows"])
                act(row("TMP", W), row("TMP", W), AF.Ln, ["rows"], ["rows"])
                act(row("S1", W), row("TMP", W), AF.Exp, ["rows"], ["rows"],
                    scale=-1.0)
                dve(lambda e: e.tensor_scalar(row("S1", W), row("S1", W),
                                              127.0, None, ALU.mult),
                    ["rows"], ["rows"])
                bcast32(RO["S1"] * COLS, W,
                        AP(bct_c, 0, [[1, 128], [1, W]]), "bct_c")
                for pc in range(ceil_div(PFF, NOC)):
                    ocs = list(range(pc * NOC, min((pc + 1) * NOC, PFF)))
                    load_piece(half, s_wd, ocs, 0, PD, "s_wd_hbm")
                    for j, oc in enumerate(ocs):
                        ir = 0
                        vr = (oc % 2) * W
                        dve(lambda e, oc=oc, ir=ir: e.tensor_tensor(
                            AP(e_ring, ir, [[1, 128], [1, W]]),
                            AP(vbuf, oc * W, [[1, 128], [1, W]]),
                            AP(bct_c, 0, [[1, 128], [1, W]]), ALU.mult),
                            ["vbuf", "bct_c"], ["e_ring"])
                        dve(lambda e, ir=ir: e.tensor_scalar(
                            AP(i32_ring, ir, [[1, 128], [1, W]]),
                            AP(e_ring, ir, [[1, 128], [1, W]]), MAGIC, MAGIC,
                            ALU.add, ALU.subtract), ["e_ring"], ["i32_ring"])
                        dve(lambda e, ir=ir, vr=vr: e.tensor_copy(
                            vq_ring[:, vr:vr + W],
                            AP(i32_ring, ir, [[1, 128], [1, W]])),
                            ["i32_ring"], ["vq_ring"])
                        mpr = max(1, 512 // W)
                        for mc in range(PD):
                            pe(lambda e, j=j, mc=mc, half=half, vr=vr, oc=oc,
                               mpr=mpr: e.matmul(
                                   psDN(mc, W),
                                   piece_tile(half, j, mc, PD),
                                   vq_ring[:, vr:vr + W],
                                   start=(oc == 0 and mc % mpr == 0),
                                   stop=(oc == PFF - 1 and
                                         (mc % mpr == mpr - 1 or
                                          mc == PD - 1))),
                               ["wp%d" % half, "vq_ring"], ["ps_dn"],
                               inc=(oc == PFF - 1 and mc == PD - 1))
                    half ^= 1
                for mc in range(PD):
                    act(AP(scr16a, mc * COLS + b * W, [[1, 128], [1, W]]),
                        psDN(mc, W), AF.Copy,
                        ["ps_dn"], ["scr16a"])
            dbg_dump("d_ffo", scr16a[:], PD * COLS, "scr16a")

            stats("scr16a", a16, PD)
            ln_rows(D)
            bcast16(row("R"), 0, bct_a[:], "bct_a")
            bcast16(row("NMR"), 1, bct_b[:], "bct_b")
            for i in range(PD):
                dve(lambda e, i=i: e.tensor_tensor(
                    b16(i), a16(i), bct_a[:], ALU.mult),
                    ["scr16a", "bct_a"], ["scr16b"])
                dve(lambda e, i=i: e.tensor_tensor(
                    b16(i), b16(i), bct_b[:], ALU.add),
                    ["scr16b", "bct_b"], ["scr16b"])
                dve(lambda e, i=i: e.tensor_tensor(
                    AP(x_res, i * COLS, [[1, 128], [1, COLS]]),
                    AP(x_res, i * COLS, [[1, 128], [1, COLS]]),
                    b16(i), ALU.add), ["x_res", "scr16b"], ["x_res"])
            for b in range(NB):
                for ts in range(NTS):
                    rws = min(128, W - ts * 128)
                    for dpb in range(PD):
                        pe(lambda e, b=b, ts=ts, dpb=dpb, rws=rws: e.transpose(
                            psT(0, rws, 128),
                            AP(x_res, dpb * COLS + b * W + ts * 128,
                               [[1, 128], [1, rws]]),
                            idn[:]), ["x_res", "idn"], ["ps_c"])
                        act(AP(stage, dpb * 128, [[1, rws], [1, 128]]),
                            psT(0, rws, 128), AF.Copy, ["ps_c"], ["stage"])
                    dma("QG",
                        x_out.ap()[b, t0 + ts * 128:t0 + ts * 128 + rws, :],
                        AP(stage, 0, [[1, rws], [1, D]]),
                        ["stage"], ["xout_hbm"])

        qg_final = [("w", f"QG{i}", tr.cnt[f"QG{i}"]) for i in range(NSLOT)
                    if tr.cnt[f"QG{i}"] > 0]
        qs_final = [("w", f"QS{i}", tr.cnt[f"QS{i}"]) for i in range(NSLOT)
                    if tr.cnt[f"QS{i}"] > 0]
        tr.records.append(("QG", qg_final, None, "QG0", 0))
        tr.records.append(("QS", qs_final, None, "QS0", 0))

        sems = {}
        for k in tr.sems:
            sems[k] = es.enter_context(nc.semaphore("sem_" + k))
        blk = es.enter_context(nc.Block())
        ENGSEL = {"PE": "tensor", "ACT": "scalar", "DVE": "vector",
                  "QS": "sync", "QG": "gpsimd"}
        by_eng = {k: [] for k in ENGSEL}
        for rec in tr.records:
            by_eng[rec[0]].append(rec)

        def run_stream(eng_obj, recs):
            waited = {}
            for eng, pre, emitfn, sem_tag, step in recs:
                for a in pre:
                    if a[0] == "d":
                        eng_obj.drain()
                    else:
                        _, sm, pc = a
                        val = pc.val if isinstance(pc, Pending) else pc
                        if val is None:
                            raise RuntimeError(
                                f"unresolved dep {sm} for {eng}")
                        if waited.get(sm, -1) < val:
                            eng_obj.wait_ge(sems[sm], val)
                            waited[sm] = val
                if emitfn == "BUMP":
                    eng_obj.sem_inc(sems[sem_tag], 1)
                    continue
                inst = emitfn(eng_obj) if emitfn is not None else None
                if step and inst is not None:
                    inst.then_inc(sems[sem_tag], step)

        @blk.tensor
        def _(e):
            run_stream(e, by_eng["PE"])

        @blk.scalar
        def _(e):
            run_stream(e, by_eng["ACT"])

        @blk.vector
        def _(e):
            run_stream(e, by_eng["DVE"])

        @blk.sync
        def _(e):
            run_stream(e, by_eng["QS"])

        @blk.gpsimd
        def _(e):
            run_stream(e, by_eng["QG"])

    return nc, dbg




_CACHE = {}


def _get_nc():
    if "nc" not in _CACHE:
        _CACHE["nc"] = build_program(full_cfg())[0]
    return _CACHE["nc"]


def kernel(L0_attn_x, L0_attn_W_in, L0_attn_b_in, L0_attn_W_xproj,
           L0_attn_b_xproj, L0_attn_W_dt, L0_attn_b_dt, L0_attn_W_out,
           L0_attn_b_out, L0_attn_A_log, L0_attn_D_param, L0_ffn_w_gate,
           L0_ffn_w_up, L0_ffn_w_down):
    import ml_dtypes
    from concourse.bass_utils import run_bass_kernel_spmd
    n = 8
    x = np.ascontiguousarray(np.asarray(L0_attn_x, dtype=np.float32))
    shards = [np.ascontiguousarray(x[2 * i:2 * i + 2]) for i in range(n)]
    common = dict(
        W_in=np.ascontiguousarray(np.asarray(L0_attn_W_in, np.float32)),
        b_in=np.ascontiguousarray(np.asarray(L0_attn_b_in, np.float32)),
        W_xp=np.ascontiguousarray(np.asarray(L0_attn_W_xproj, np.float32)),
        b_xp=np.ascontiguousarray(np.asarray(L0_attn_b_xproj, np.float32)),
        W_dt=np.ascontiguousarray(np.asarray(L0_attn_W_dt, np.float32)),
        b_dt=np.ascontiguousarray(np.asarray(L0_attn_b_dt, np.float32)),
        W_out=np.ascontiguousarray(np.asarray(L0_attn_W_out, np.float32)),
        b_out=np.ascontiguousarray(np.asarray(L0_attn_b_out, np.float32)),
        A_log=np.ascontiguousarray(np.asarray(L0_attn_A_log, np.float32)),
        D_par=np.ascontiguousarray(np.asarray(L0_attn_D_param, np.float32)),
        w_g=np.ascontiguousarray(np.asarray(L0_ffn_w_gate, np.float32)),
        w_u=np.ascontiguousarray(np.asarray(L0_ffn_w_up, np.float32)),
        w_d=np.ascontiguousarray(np.asarray(L0_ffn_w_down, np.float32)),
        ident=np.eye(128, dtype=np.float32),
        identb=np.eye(128, dtype=ml_dtypes.bfloat16),
    )
    in_maps = [dict(common, x_in=shards[i]) for i in range(n)]
    nc = _get_nc()
    res = run_bass_kernel_spmd(nc, in_maps, core_ids=list(range(n)))
    out = np.concatenate([r["x_out"] for r in res.results], axis=0)
    return np.ascontiguousarray(out.astype(np.float32))


# version 03 (final)
# speedup vs baseline: 2.7209x; 2.7209x over previous
import numpy as np
import concourse.bass as bass
import concourse.mybir as mybir

F32 = mybir.dt.float32
BF16 = mybir.dt.bfloat16
I32 = mybir.dt.int32
AF = mybir.ActivationFunctionType
ALU = mybir.AluOpType
X = mybir.AxisListType.X
MAGIC = 1.5 * 2.0**23


def full_cfg():
    return dict(D=1024, DI=2048, DS=16, DFF=4096, NB=2, T=4096, W=256,
                debug=False)


def small_cfg():
    return dict(D=256, DI=512, DS=4, DFF=512, NB=2, T=128, W=64, debug=True)


class Pending:
    __slots__ = ("val",)

    def __init__(self):
        self.val = None


NSLOT = 8


class Track:

    DMA_Q = ("QS", "QG")

    def __init__(self):
        engs = ["PE", "ACT", "DVE", "GP"]
        self.sems = list(engs)
        for q in self.DMA_Q:
            self.sems += [f"{q}{i}" for i in range(NSLOT)]
        self.cnt = {k: 0 for k in self.sems}
        self.rr = {q: 0 for q in self.DMA_Q}
        self.last_w = {}
        self.readers = {}
        self.op_seq = {k: 0 for k in ["PE", "ACT", "DVE", "GP",
                                      "QS", "QG"]}
        self.drain_seq = {k: -1 for k in ["PE", "ACT", "DVE", "GP",
                                          "QS", "QG"]}
        self.pending = {k: [] for k in ["PE", "ACT", "DVE", "GP"]}
        self.records = []
        self.labels = []
        self.section = "?"
        self.read_hook = None
        self.ev_idx = {}
        self.pending_idx = {}

    def _resolve(self, eng):
        for p in self.pending[eng]:
            p.val = self.cnt[eng]
        self.pending[eng] = []

    def emit(self, eng, emitfn, reads=(), writes=(), inc=True):
        pre = []
        need_drain = False
        is_q = eng in self.DMA_Q

        def wait_on(sem, pc, peng, pseq, raw=True):
            nonlocal need_drain
            if peng == eng and not is_q:
                if eng != "PE" and pseq > self.drain_seq[eng]:
                    need_drain = True
                return
            if peng == eng and is_q:
                pre.append(("w", sem, pc))
                return
            if isinstance(pc, Pending) and pc.val is None:
                self.cnt[sem] += 1
                self._resolve(sem)
                self.ev_idx[(sem, self.cnt[sem])] = len(self.records)
                self.records.append((sem, [("d",)], "BUMP", sem, 1))
                self.labels.append("BUMP")
                self.op_seq[sem] += 1
                self.drain_seq[sem] = self.op_seq[sem] - 1
                pc = self.cnt[sem]
            pre.append(("w", sem, pc))

        for r in reads:
            if self.read_hook is not None:
                self.read_hook(r)
            lw = self.last_w.get(r)
            if lw is not None:
                wait_on(*lw)
        for w in writes:
            lw = self.last_w.get(w)
            if lw is not None:
                wait_on(*lw, raw=False)
            for sem, (pc, peng, pseq) in self.readers.get(w, {}).items():
                wait_on(sem, pc, peng, pseq, raw=False)
        if need_drain:
            pre.insert(0, ("d",))
            self.drain_seq[eng] = self.op_seq[eng] - 1
        myseq = self.op_seq[eng]
        self.op_seq[eng] += 1
        if is_q:
            slot = f"{eng}{self.rr[eng]}"
            self.rr[eng] = (self.rr[eng] + 1) % NSLOT
            if self.cnt[slot] > 0:
                pre.append(("w", slot, self.cnt[slot]))
            self.cnt[slot] += 16
            c = self.cnt[slot]
            sem_tag, step = slot, 16
        elif inc:
            self.cnt[eng] += 1
            self._resolve(eng)
            c = self.cnt[eng]
            sem_tag, step = eng, 1
        else:
            c = Pending()
            self.pending[eng].append(c)
            sem_tag, step = eng, 0
        for w in writes:
            self.last_w[w] = (sem_tag, c, eng, myseq)
            self.readers[w] = {}
        for r in reads:
            self.readers.setdefault(r, {})[
                eng if not is_q else sem_tag] = (c, eng, myseq)
        if step:
            self.ev_idx[(sem_tag, c)] = len(self.records)
        else:
            self.pending_idx.setdefault(eng, []).append(
                (c, len(self.records)))
        self.records.append((eng, pre, emitfn, sem_tag, step))
        self.labels.append(self.section)

    def verify_order(self):
        bad = 0
        for idx, (eng, pre, _, _, _) in enumerate(self.records):
            for a in pre:
                if a[0] != "w":
                    continue
                _, sm, pc = a
                val = pc.val if isinstance(pc, Pending) else pc
                pidx = self.ev_idx.get((sm, val))
                if pidx is None or pidx > idx:
                    print(f"FORWARD EDGE: record {idx} ({eng}) waits "
                          f"{sm}>={val} produced at record {pidx}")
                    bad += 1
                    if bad > 20:
                        return


def ceil_div(a, b):
    return (a + b - 1) // b


def build_program(cfg):
    D, DI, DS, DFF = cfg["D"], cfg["DI"], cfg["DS"], cfg["DFF"]
    NB, T, W = cfg["NB"], cfg["T"], cfg["W"]
    debug = cfg.get("debug", False)
    COLS = NB * W
    NPASS = T // W
    PD, PDI, PFF = D // 128, DI // 128, DFF // 128
    NTS = ceil_div(W, 128)
    assert COLS <= 512

    nc = bass.Bass("TRN2", target_bir_lowering=False)

    x_in = nc.dram_tensor("x_in", [NB, T, D], F32, kind="ExternalInput")
    W_in = nc.dram_tensor("W_in", [D, 2 * DI], F32, kind="ExternalInput")
    b_in = nc.dram_tensor("b_in", [2 * DI], F32, kind="ExternalInput")
    W_xp = nc.dram_tensor("W_xp", [DI, 2 * DS], F32, kind="ExternalInput")
    b_xp = nc.dram_tensor("b_xp", [2 * DS], F32, kind="ExternalInput")
    W_dt = nc.dram_tensor("W_dt", [DI, DI], F32, kind="ExternalInput")
    b_dt = nc.dram_tensor("b_dt", [DI], F32, kind="ExternalInput")
    W_out = nc.dram_tensor("W_out", [DI, D], F32, kind="ExternalInput")
    b_out = nc.dram_tensor("b_out", [D], F32, kind="ExternalInput")
    A_log = nc.dram_tensor("A_log", [DI, DS], F32, kind="ExternalInput")
    D_par = nc.dram_tensor("D_par", [DI], F32, kind="ExternalInput")
    w_g = nc.dram_tensor("w_g", [DFF, D], F32, kind="ExternalInput")
    w_u = nc.dram_tensor("w_u", [DFF, D], F32, kind="ExternalInput")
    w_d = nc.dram_tensor("w_d", [D, DFF], F32, kind="ExternalInput")
    ident = nc.dram_tensor("ident", [128, 128], F32, kind="ExternalInput")
    identb = nc.dram_tensor("identb", [128, 128], BF16, kind="ExternalInput")
    x_out = nc.dram_tensor("x_out", [NB, T, D], F32, kind="ExternalOutput")

    def scratch(name, K, M, pc_tiles):
        npc = (K // 128) * (M // 128) // pc_tiles
        return nc.dram_tensor(name, [npc, 128, pc_tiles * 128], BF16,
                              kind="Internal")
    s_Win = scratch("s_Win", D, 2 * DI, PD)
    s_Wxp = nc.dram_tensor("s_Wxp", [PDI, 128, 2 * DS], BF16, kind="Internal")
    s_Wdt = scratch("s_Wdt", DI, DI, PDI // 2)
    s_Wout = scratch("s_Wout", DI, D, PDI // 2)
    s_wg = scratch("s_wg", D, DFF, PD)
    s_wu = scratch("s_wu", D, DFF, PD)
    s_wd = scratch("s_wd", DFF, D, PD)

    dbg = {}
    if debug:
        for name, shape in [
            ("d_xn", [128, PD * COLS]), ("d_u", [128, PDI * COLS]),
            ("d_sz", [128, PDI * COLS]), ("d_dt", [128, PDI * COLS]),
            ("d_yg", [128, PDI * COLS]), ("d_mo", [128, PD * COLS]),
            ("d_x1", [128, PD * COLS]), ("d_xqs", [128, PD * COLS]),
            ("d_v", [128, PFF * W]), ("d_ffo", [128, PD * COLS]),
        ]:
            dbg[name] = nc.dram_tensor(name, shape, F32, kind="ExternalOutput")

    from contextlib import ExitStack
    es = ExitStack()
    _pitch = {}

    def S(name, shape, dt):
        h = es.enter_context(nc.sbuf_tensor(name, shape, dt))
        _pitch[id(h)] = shape[1]
        return h

    with es:
        x_res = S("x_res", [128, PD * COLS], BF16)
        u = S("u", [128, PDI * COLS], BF16)
        sz = S("sz", [128, PDI * COLS], BF16)
        dt_t = S("dt_t", [128, PDI * COLS], BF16)
        xqs = S("xqs", [128, PD * COLS], BF16)
        vbuf = S("vbuf", [128, PFF * W], BF16)
        stage = S("stage", [128, D], F32)
        scr16a = S("scr16a", [128, PD * COLS], BF16)
        scr16b = S("scr16b", [128, PD * COLS], BF16)
        scr16c = S("scr16c", [128, PD * COLS], BF16)
        SH = max(1, DS // 2)
        dA = S("dA", [128, SH * COLS], BF16)
        Harena = S("Harena", [128, SH * W], BF16)
        Parena = S("Parena", [128, SH * W], BF16)
        ypart = S("ypart", [128, COLS], BF16)
        wtile = S("wtile", [128, COLS], BF16)
        Bfull = S("Bfull", [128, DS * COLS], BF16)
        Cfull = S("Cfull", [128, DS * COLS], BF16)
        colmax = S("colmax", [128, COLS + W], F32)
        colmin = S("colmin", [128, COLS + W], BF16)
        carry = S("carry", [128, NB * PDI * DS], F32)
        e_ring = S("e_ring", [128, 2 * COLS + W], F32)
        sq_ring = S("sq_ring", [128, COLS], BF16)
        sq2 = S("sq2", [128, 2 * COLS], BF16)
        g_ring = S("g_ring", [128, 2 * W], BF16)
        vq_ring = S("vq_ring", [128, 2 * W], BF16)
        i32_ring = S("i32_ring", [128, COLS + W], I32)
        rows = S("rows", [1, 12 * COLS], BF16)
        bct = S("bct", [128, 6 * COLS], BF16)
        wp = S("wp", [128, 2 * 1024], BF16)
        wpf = S("wpf", [128, 3 * 1024], BF16)
        idn = S("idn", [128, 128], F32)
        Acols = S("Acols", [128, PDI * DS], F32)
        bcol_in = S("bcol_in", [128, 2 * PDI], F32)
        bcol_dt = S("bcol_dt", [128, PDI], F32)
        bcol_out = S("bcol_out", [128, PD], F32)
        bcol_xp = S("bcol_xp", [2 * DS, 1], F32)
        Dcols = S("Dcols", [128, PDI], F32)
        mw_col = S("mw_col", [128, 1], F32)
        ones16 = S("ones16", [128, 1], BF16)
        ones32 = S("ones32", [128, 1], F32)
        c_eps = S("c_eps", [128, 1], F32)
        c_one = S("c_one", [128, 1], F32)
        scal = S("scal", [1, 8], F32)
        qsum = S("qsum", [128, 8], F32)
        if debug:
            dstage = S("dstage", [128, max(PDI, PFF) * COLS], F32)
        psum = es.enter_context(nc.psum_tensor("psum", [128, 4096], F32))

        tr = Track()

        mmb_state = [0]
        dn_open = [False]
        bank_await = set()
        tr.read_hook = bank_await.discard

        def next_mmbank():
            for _ in range(8):
                i = mmb_state[0] % 8
                mmb_state[0] = (mmb_state[0] + 1) % 8
                if dn_open[0] and i >= 4:
                    continue
                nm = "mmb%d" % i
                if nm in bank_await:
                    continue
                bank_await.add(nm)
                return 512 * i, nm
            raise RuntimeError("all psum banks awaiting readers")

        def psDN(mc, n):
            return psum[:, 2048 + mc * n:2048 + (mc + 1) * n]

        def dma(q, out_ap, in_ap, reads, writes, inc=True):
            tr.emit(q, lambda e: e.dma_start(
                out=out_ap, in_=in_ap, allow_slow_non_contiguous=True),
                reads, writes, inc)

        def act(out_ap, in_ap, func, reads, writes, bias=0.0, scale=1.0,
                inc=True):
            tr.emit("ACT",
                    lambda e: e.activation(out_ap, in_ap, func, bias=bias,
                                           scale=scale),
                    reads, writes, inc)

        def gp(emitfn, reads, writes, inc=True):
            tr.emit("GP", emitfn, reads, writes, inc)

        def dve(emitfn, reads, writes, inc=True):
            tr.emit("DVE", emitfn, reads, writes, inc)

        def pe(emitfn, reads, writes, inc=True):
            tr.emit("PE", emitfn, reads, writes, inc)

        def AP(t, off, ap):
            p = _pitch.get(id(t))
            if p is not None:
                ap = [[p, ap[0][1]]] + [list(x) for x in ap[1:]]
            return bass.AP(t, off, ap)

        dma("QS", idn[:], ident.ap(), [], ["idn"])
        dma("QS", bcol_in[:], AP(b_in, 0, [[1, 128], [128, 2 * PDI]]),
            [], ["bcol_in"])
        dma("QS", bcol_dt[:], AP(b_dt, 0, [[1, 128], [128, PDI]]),
            [], ["bcol_dt"])
        dma("QS", bcol_out[:], AP(b_out, 0, [[1, 128], [128, PD]]),
            [], ["bcol_out"])
        dma("QS", bcol_xp[:], AP(b_xp, 0, [[1, 2 * DS], [1, 1]]),
            [], ["bcol_xp"])
        dma("QS", Dcols[:], AP(D_par, 0, [[1, 128], [128, PDI]]),
            [], ["Dcols"])
        dma("QS", Acols[:], AP(A_log, 0, [[DS, 128], [128 * DS, PDI], [1, DS]]),
            [], ["Acols"])
        act(Acols[:], Acols[:], AF.Exp, ["Acols"], ["Acols"])
        dve(lambda e: e.tensor_scalar(Acols[:], Acols[:], -1.0, None,
                                      ALU.mult), ["Acols"], ["Acols"])
        NSH_ = (DS + SH - 1) // SH
        dve(lambda e: e.memset(carry[:], 0.0), [],
            ["cr:%d:%d:%d" % (b_, k_, sh_) for b_ in range(NB)
             for k_ in range(PDI) for sh_ in range(NSH_)])
        dve(lambda e: e.memset(ones16[:], 1.0), [], ["ones16"])
        dve(lambda e: e.memset(ones32[:], 1.0), [], ["ones32"])
        dve(lambda e: e.memset(c_eps[:], 1e-5), [], ["c_eps"])
        dve(lambda e: e.memset(c_one[:], 1.0), [], ["c_one"])

        CHK = min(512, D // 2)
        WVS = [(stage, 0), (stage, CHK)]
        for _i in range(min(2, (2 * COLS) // CHK)):
            WVS.append((e_ring, _i * CHK))
        if COLS >= CHK:
            WVS.append((colmax, 0))
        NSW = len(WVS)
        NS16 = min(8, (PD * COLS) // CHK)
        NST = min(8, (PD * COLS) // 128)
        wv_i, s16_i, st_i = [0], [0], [0]

        def wv_next():
            i = wv_i[0]
            wv_i[0] = (i + 1) % NSW
            return i

        def wv_ap(i, n):
            assert n <= CHK
            t, off = WVS[i]
            return AP(t, off, [[1, 128], [1, n]])

        def s16_next():
            i = s16_i[0]
            s16_i[0] = (i + 1) % NS16
            return i

        def s16_ap(i, n):
            return AP(scr16b, i * CHK, [[1, 128], [1, n]])

        def st_next():
            i = st_i[0]
            st_i[0] = (i + 1) % NST
            return i

        def st_ap(i):
            return AP(scr16c, i * 128, [[1, 128], [1, 128]])

        def prepass_cast(src, dst, K, M, kc_per_piece):
            tr.section = "prep_c_" + dst.name
            nkc = K // 128
            kpp = kc_per_piece
            npk = nkc // kpp
            for kc in range(nkc):
                for c0 in range(0, M, CHK):
                    cw = min(CHK, M - c0)
                    h = wv_next()
                    s = s16_next()
                    wn, sn = "ppw%d" % h, "pps%d" % s
                    dma("QS", wv_ap(h, cw),
                        src.ap()[kc * 128:(kc + 1) * 128, c0:c0 + cw],
                        [wn], [wn])
                    act(s16_ap(s, cw), wv_ap(h, cw), AF.Copy, [wn], [sn])
                    nmc_chunk = cw // 128
                    mc0 = c0 // 128
                    piece0 = mc0 * npk + kc // kpp
                    pcols = kpp * 128
                    coff = (kc % kpp) * 128
                    dma("QS",
                        bass.AP(dst, piece0 * 128 * pcols + coff,
                                [[pcols, 128], [npk * 128 * pcols, nmc_chunk],
                                 [1, 128]]),
                        AP(scr16b, s * CHK,
                           [[1, 128], [128, nmc_chunk], [1, 128]]),
                        [sn], [dst.name + "_hbm"])

        def prepass_quant(src, dst, R, Cc, scal_idx, wd_mode=False):
            tr.section = "prep_qA_" + dst.name
            nblk = R // 128
            nch = ceil_div(Cc, CHK)
            dve(lambda e: e.memset(scal[0:1, 3:4], 0.0), [], ["scal"])
            for kc in range(nblk):
                for ci in range(nch):
                    c0 = ci * CHK
                    cw = min(CHK, Cc - c0)
                    qi = kc * nch + ci
                    h = wv_next()
                    wn = "ppw%d" % h
                    dma("QS", wv_ap(h, cw),
                        src.ap()[kc * 128:(kc + 1) * 128, c0:c0 + cw],
                        [wn], [wn])
                    act(wv_ap(h, cw), wv_ap(h, cw), AF.Abs, [wn], [wn])
                    dve(lambda e, qi=qi, cw=cw, h=h: e.tensor_reduce(
                        qsum[:, qi % 8:qi % 8 + 1], wv_ap(h, cw), X,
                        ALU.add), [wn], ["qsum"])
                    if qi % 8 == 7 or (kc == nblk - 1 and ci == nch - 1):
                        n = qi % 8 + 1
                        mb, mn = next_mmbank()
                        pe(lambda e, n=n, mb=mb: e.matmul(
                            psum[0:1, mb:mb + n], ones32[:], qsum[:, 0:n],
                            start=True, stop=True),
                           ["ones32", "qsum"], [mn])
                        dve(lambda e, n=n, mb=mb, qi=qi: e.tensor_reduce(
                            scal[0:1, 4:5] if qi > 7 else scal[0:1, 3:4],
                            psum[0:1, mb:mb + n], X, ALU.add),
                            [mn], ["scal"])
                        if qi > 7:
                            dve(lambda e: e.tensor_tensor(
                                scal[0:1, 3:4], scal[0:1, 3:4],
                                scal[0:1, 4:5], ALU.add), ["scal"], ["scal"])
            dve(lambda e: e.tensor_scalar(
                scal[0:1, 0:1], scal[0:1, 3:4], 1.0 / (R * Cc), 1e-5,
                ALU.mult, ALU.max), ["scal"], ["scal"])
            dve(lambda e: e.memset(scal[0:1, 3:4], 0.0), [], ["scal"])
            if scal_idx == 0:
                dve(lambda e: e.tensor_copy(scal[0:1, 2:3], scal[0:1, 0:1]),
                    ["scal"], ["scal"])
            act(scal[0:1, 1:2], scal[0:1, 0:1], AF.Ln, ["scal"], ["scal"])
            act(scal[0:1, 1:2], scal[0:1, 1:2], AF.Exp, ["scal"], ["scal"],
                scale=-1.0)
            dma("QG", mw_col[:], AP(scal, 1, [[1, 1], [0, 128], [1, 1]]),
                ["scal"], ["mw_col"])
            tr.section = "prep_qB_" + dst.name
            for kc in range(nblk):
                for ci in range(nch):
                    c0 = ci * CHK
                    cw = min(CHK, Cc - c0)
                    h = wv_next()
                    wn = "ppw%d" % h
                    dma("QS", wv_ap(h, cw),
                        src.ap()[kc * 128:(kc + 1) * 128, c0:c0 + cw],
                        [wn, dst.name + "_hbm"], [wn])
                    dve(lambda e, cw=cw, h=h: e.tensor_scalar(
                        wv_ap(h, cw), wv_ap(h, cw), mw_col[:, 0:1], None,
                        ALU.mult), [wn, "mw_col"], [wn])
                    dve(lambda e, cw=cw, h=h: e.tensor_scalar(
                        wv_ap(h, cw), wv_ap(h, cw), MAGIC, MAGIC, ALU.add,
                        ALU.subtract), [wn], [wn])
                    dve(lambda e, cw=cw, h=h: e.tensor_scalar(
                        wv_ap(h, cw), wv_ap(h, cw), 1.0, -1.0, ALU.min,
                        ALU.max), [wn], [wn])
                    for f0 in range(0, cw, COLS):
                        fw = min(COLS, cw - f0)
                        dve(lambda e, f0=f0, fw=fw, h=h: e.tensor_copy(
                            AP(i32_ring, 0, [[1, 128], [1, fw]]),
                            AP(WVS[h][0], WVS[h][1] + f0,
                               [[1, 128], [1, fw]])),
                            [wn], ["i32_ring"])
                        dve(lambda e, f0=f0, fw=fw, h=h: e.tensor_copy(
                            AP(WVS[h][0], WVS[h][1] + f0,
                               [[1, 128], [1, fw]]),
                            AP(i32_ring, 0, [[1, 128], [1, fw]])),
                            ["i32_ring"], [wn])
                    for t in range(cw // 128):
                        ck = (c0 + t * 128) // 128
                        ts_ = st_next()
                        tn = "ppt%d" % ts_
                        mbt, mnt = next_mmbank()
                        pe(lambda e, t=t, h=h, mbt=mbt: e.transpose(
                            psum[0:128, mbt:mbt + 128],
                            AP(WVS[h][0], WVS[h][1] + t * 128,
                               [[1, 128], [1, 128]]),
                            idn[:]), [wn, "idn"], [mnt])
                        act(st_ap(ts_), psum[0:128, mbt:mbt + 128], AF.Copy,
                            [mnt], [tn])
                        pcols2 = dst.shape[2]
                        if wd_mode:
                            piece, coff2 = ck, kc * 128
                        else:
                            piece, coff2 = kc, ck * 128
                        dma("QS",
                            bass.AP(dst, piece * 128 * pcols2 + coff2,
                                    [[pcols2, 128], [1, 128]]),
                            st_ap(ts_), [tn], [dst.name + "_hbm"])

        tr.section = "prep"
        prepass_cast(W_in, s_Win, D, 2 * DI, PD)
        tr.section = "prep_xp"
        for kc in range(PDI):
            h = wv_next()
            s = s16_next()
            wn, sn = "ppw%d" % h, "pps%d" % s
            dma("QS", wv_ap(h, 2 * DS),
                W_xp.ap()[kc * 128:(kc + 1) * 128, :], [wn], [wn])
            act(s16_ap(s, 2 * DS), wv_ap(h, 2 * DS), AF.Copy, [wn], [sn])
            dma("QS", s_Wxp.ap()[kc, :, :], s16_ap(s, 2 * DS),
                [sn], ["s_Wxp_hbm"])
        prepass_cast(W_dt, s_Wdt, DI, DI, PDI // 2)
        prepass_cast(W_out, s_Wout, DI, D, PDI // 2)
        prepass_quant(w_g, s_wg, DFF, D, 0)
        prepass_quant(w_u, s_wu, DFF, D, 1)
        prepass_quant(w_d, s_wd, D, DFF, 2, wd_mode=True)
        dma("QG", mw_col[:], AP(scal, 2, [[1, 1], [0, 128], [1, 1]]),
            ["scal"], ["mw_col"])

        wp_state = [0]
        wpf_state = [0]

        def load_piece(src, piece, tag):
            half = wp_state[0]
            wp_state[0] ^= 1
            hb = "wp%d" % half
            dma("QS", AP(wp, half * 1024, [[1, 128], [1, src.shape[2]]]),
                src.ap()[piece, :, :], [hb, tag], [hb])
            return half, hb

        def piece_tile(half, i):
            return AP(wp, half * 1024 + i * 128, [[1, 128], [1, 128]])

        def load_piece_f(src, piece, tag):
            sl = wpf_state[0]
            wpf_state[0] = (sl + 1) % 3
            hb = "wpf%d" % sl
            dma("QS", AP(wpf, sl * 1024, [[1, 128], [1, src.shape[2]]]),
                src.ap()[piece, :, :], [hb, tag], [hb])
            return sl, hb

        def piece_tile_f(sl, i):
            return AP(wpf, sl * 1024 + i * 128, [[1, 128], [1, 128]])

        RO = dict(MU=0, VAR=1, TMP=2, R=3, NMR=4, M1=5)
        RO["TMP2"] = RO["VAR"]
        RO["S1INV"] = RO["MU"]
        RO["S1"] = RO["M1"]
        BCT = dict(R=0, NMR=1, S1=2, S1INV=3)

        def row(grp, name, n=None):
            o = (grp * 6 + RO[name]) * COLS
            return rows[0:1, o:o + (n or COLS)]

        def bct_off(grp, name):
            if BCT[name] < 2:
                return (grp * 2 + BCT[name]) * COLS
            if name == "S1":
                return (4 + grp) * COLS
            return (grp * 2 + 0) * COLS

        def bct_t(grp, name, n=None):
            o = bct_off(grp, name)
            return bct[:, o:o + (n or COLS)]

        def bct_nm(grp, name):
            if name == "S1INV":
                return "bct_S1INV"
            return "bct%d_%s" % (grp, name)

        def stats(srcname, src_t, src_off, nblk, sq_slot):
            mb1, mn1 = next_mmbank()
            for i in range(nblk):
                pe(lambda e, i=i, mb1=mb1: e.matmul(
                    psum[0:1, mb1:mb1 + COLS], ones16[:],
                    AP(src_t, src_off + i * COLS, [[1, 128], [1, COLS]]),
                    start=(i == 0), stop=(i == nblk - 1)),
                   [srcname, "ones16"], [mn1], inc=(i == nblk - 1))
            mb2, mn2 = next_mmbank()
            sqn = "sq2_%d" % sq_slot
            for i in range(nblk):
                act(AP(sq2, sq_slot * COLS, [[1, 128], [1, COLS]]),
                    AP(src_t, src_off + i * COLS, [[1, 128], [1, COLS]]),
                    AF.Square, [srcname], [sqn])
                pe(lambda e, i=i, mb2=mb2: e.matmul(
                    psum[0:1, mb2:mb2 + COLS], ones16[:],
                    AP(sq2, sq_slot * COLS, [[1, 128], [1, COLS]]),
                    start=(i == 0), stop=(i == nblk - 1)),
                   [sqn, "ones16"], [mn2], inc=(i == nblk - 1))
            return (mb1, mn1, mb2, mn2)

        def ln_rows(grp, Dn, st):
            mb1, mn1, mb2, mn2 = st
            act(row(grp, "MU"), psum[0:1, mb1:mb1 + COLS], AF.Copy,
                [mn1], ["rows"], scale=1.0 / Dn)
            act(row(grp, "VAR"), psum[0:1, mb2:mb2 + COLS], AF.Copy,
                [mn2], ["rows"], scale=1.0 / Dn)
            dve(lambda e: e.scalar_tensor_tensor(
                row(grp, "TMP"), row(grp, "MU"), 1.0, row(grp, "MU"),
                ALU.bypass, ALU.mult), ["rows"], ["rows"])
            dve(lambda e: e.tensor_tensor(row(grp, "VAR"), row(grp, "VAR"),
                                          row(grp, "TMP"), ALU.subtract),
                ["rows"], ["rows"])
            act(row(grp, "R"), row(grp, "VAR"), AF.Ln, ["rows", "c_eps"],
                ["rows"], bias=c_eps[0:1, 0:1])
            act(row(grp, "R"), row(grp, "R"), AF.Exp, ["rows"], ["rows"],
                scale=-0.5)
            dve(lambda e: e.scalar_tensor_tensor(
                row(grp, "NMR"), row(grp, "MU"), -1.0, row(grp, "R"),
                ALU.mult, ALU.mult), ["rows"], ["rows"])

        def bcast_row(grp, name, n=None):
            o = (grp * 6 + RO[name]) * COLS
            nn = n or COLS
            bo = bct_off(grp, name)
            dma("QG", bct[:, bo:bo + nn],
                AP(rows, o, [[1, 1], [0, 128], [1, nn]]),
                ["rows"], [bct_nm(grp, name)])

        def bct0(grp, name, nblk):
            o = bct_off(grp, name)
            return bass.AP(bct, o, [[6 * COLS, 128], [0, nblk], [1, COLS]])

        def ln_apply_ip(grp, t16, t16n, nblk):
            bcast_row(grp, "R")
            bcast_row(grp, "NMR")
            dve(lambda e: e.tensor_tensor(
                t16[:, 0:nblk * COLS], t16[:, 0:nblk * COLS],
                bct0(grp, "R", nblk), ALU.mult),
                [t16n, bct_nm(grp, "R")], [t16n])
            dve(lambda e: e.tensor_tensor(
                t16[:, 0:nblk * COLS], t16[:, 0:nblk * COLS],
                bct0(grp, "NMR", nblk), ALU.add),
                [t16n, bct_nm(grp, "NMR")], [t16n])

        def quant_rows(grp, wlen):
            dve(lambda e: e.tensor_scalar(row(grp, "TMP", wlen),
                                          row(grp, "M1", wlen), 1e-5, None,
                                          ALU.max), ["rows"], ["rows"])
            act(row(grp, "TMP2", wlen), row(grp, "TMP", wlen), AF.Ln,
                ["rows"], ["rows"])
            act(row(grp, "S1", wlen), row(grp, "TMP2", wlen), AF.Exp,
                ["rows"], ["rows"], scale=-1.0)
            dve(lambda e: e.tensor_scalar(row(grp, "S1", wlen),
                                          row(grp, "S1", wlen), 127.0, None,
                                          ALU.mult), ["rows"], ["rows"])

        def colmax_to_m1(grp, coff, wlen, cm_names):
            dve(lambda e: e.tensor_scalar(
                colmin[:, coff:coff + wlen], colmin[:, coff:coff + wlen],
                -1.0, None, ALU.mult), [cm_names[1]], [cm_names[1]])
            dve(lambda e: e.tensor_tensor(
                colmax[:, coff:coff + wlen], colmax[:, coff:coff + wlen],
                colmin[:, coff:coff + wlen], ALU.max),
                cm_names, [cm_names[0]])
            ncs = ceil_div(wlen, 128)
            for cs in range(ncs):
                cw = min(128, wlen - cs * 128)
                mbt, mnt = next_mmbank()
                pe(lambda e, cs=cs, cw=cw, mbt=mbt: e.transpose(
                    psum[0:cw, mbt:mbt + 128],
                    colmax[:, coff + cs * 128:coff + cs * 128 + cw],
                    idn[:]), [cm_names[0], "idn"], [mnt])
                dve(lambda e, cs=cs, cw=cw, mbt=mbt: e.tensor_reduce(
                    qsum[0:cw, cs:cs + 1], psum[0:cw, mbt:mbt + 128], X,
                    ALU.max), [mnt], ["qsum"])
            cw0 = min(128, wlen)
            for cs in range(ncs):
                dma("QG", AP(rows, (grp * 6 + RO["M1"]) * COLS + cs * 128,
                             [[1, 1], [1, cw0]]),
                    qsum[0:cw0, cs:cs + 1], ["qsum"], ["rows"])

        DBG_PASS = cfg.get("dbg_pass", None)

        def dbg_tap(key, t, off, n, readname, p_=None):
            if not debug:
                return
            if DBG_PASS is not None and p_ is not None and p_ != DBG_PASS:
                return
            act(AP(dstage, 0, [[1, 128], [1, n]]),
                AP(t, off, [[1, 128], [1, n]]), AF.Copy,
                [readname], ["dstage"])
            dma("QG", dbg[key].ap(), AP(dstage, 0, [[1, 128], [1, n]]),
                ["dstage"], [key])

        def nmf(base, *idx):
            return base + ":" + ":".join(str(i) for i in idx)

        def UB(k, b):
            return nmf("u", k, b)

        def DT(k):
            return nmf("dt", k)

        def SZ(k):
            return nmf("sz", k)

        def DTBUF(p):
            return dt_t if p % 2 == 0 else sz

        def SZBUF(p):
            return sz if p % 2 == 0 else dt_t

        def BUFN(buf, k):
            return nmf("bA" if buf is dt_t else "bB", k)

        def XQ(k):
            return nmf("xqs", k)

        def VB(k):
            return nmf("vb", k)

        def gen_FR(p):
            grp = p % 2
            t0 = p * W
            extra = ["pps%d" % i for i in range(NS16)] if p == 0 else []
            tr.section = "FRxload@%d" % p
            for b in range(NB):
                for ts in range(NTS):
                    rws = min(128, W - ts * 128)
                    dma("QS", AP(stage, 0, [[1, rws], [1, D]]),
                        x_in.ap()[b, t0 + ts * 128:t0 + ts * 128 + rws, :],
                        ["stage"],
                        ["stage"] + (["ppw0", "ppw1"]
                                     if p == 0 and b == 0 and ts == 0
                                     else []))
                    for g0 in range(0, PD, 4):
                        gn = min(4, PD - g0)
                        mbt, mnt = next_mmbank()
                        for q in range(gn):
                            dpb = g0 + q
                            pe(lambda e, dpb=dpb, rws=rws, mbt=mbt, q=q,
                               gn=gn: e.matmul(
                                   psum[0:128,
                                        mbt + q * rws:mbt + (q + 1) * rws],
                                   AP(stage, dpb * 128, [[1, rws], [1, 128]]),
                                   idn[0:rws, 0:rws],
                                   start=(q == 0), stop=(q == gn - 1)),
                               ["stage", "idn"], [mnt], inc=True)
                        act(AP(scr16b, g0 * COLS + b * W + ts * 128,
                               [[1, 128], [COLS, gn], [1, rws]]),
                            psum[0:128, mbt:mbt + gn * rws], AF.Copy, [mnt],
                            ["scr16b"] + extra)
                        extra = []
                    yield
            tr.section = "FRln1@%d" % p
            st = stats("scr16b", scr16b, 0, PD, grp)
            ln_rows(grp, D, st)
            ln_apply_ip(grp, scr16b, "scr16b", PD)
            dbg_tap("d_xn", scr16b, 0, PD * COLS, "scr16b", p)
            yield
            dtw, szw = DTBUF(p), SZBUF(p)
            tr.section = "FRwin@%d" % p
            for mc in range(PDI):
                if p > 0:
                    yield ("COND", ("scan", p - 1), mc + 1)
                half, hb = load_piece(s_Win, mc, "s_Win_hbm")
                mb, mn = next_mmbank()
                for kc in range(PD):
                    pe(lambda e, kc=kc, half=half, mb=mb: e.matmul(
                        psum[:, mb:mb + COLS], piece_tile(half, kc),
                        AP(scr16b, kc * COLS, [[1, 128], [1, COLS]]),
                        start=(kc == 0), stop=(kc == PD - 1)),
                       [hb, "scr16b"], [mn], inc=(kc == PD - 1))
                act(AP(u, mc * COLS, [[1, 128], [1, COLS]]),
                    psum[:, mb:mb + COLS], AF.Silu, [mn, "bcol_in"],
                    [UB(mc, 0), UB(mc, 1)],
                    bias=bcol_in[:, mc:mc + 1])
                if mc % 4 == 3:
                    yield
            tr.section = "FRxp@%d" % p
            mb, mn = next_mmbank()
            for kc in range(PDI):
                half, hb = load_piece(s_Wxp, kc, "s_Wxp_hbm")
                pe(lambda e, kc=kc, half=half, mb=mb: e.matmul(
                    psum[0:2 * DS, mb:mb + COLS],
                    AP(wp, half * 1024, [[1, 128], [1, 2 * DS]]),
                    AP(u, kc * COLS, [[1, 128], [1, COLS]]),
                    start=(kc == 0), stop=(kc == PDI - 1)),
                   [hb, UB(kc, 0), UB(kc, 1)], [mn], inc=(kc == PDI - 1))
            act(AP(sq_ring, 0, [[1, 2 * DS], [1, COLS]]),
                psum[0:2 * DS, mb:mb + COLS], AF.Identity,
                [mn, "bcol_xp"], ["sq_ring"], bias=bcol_xp[:, 0:1])
            yield
            tr.section = "FRdt@%d" % p
            for mc in range(PDI):
                mb, mn = next_mmbank()
                for kh in range(2):
                    kcs = list(range(kh * (PDI // 2), (kh + 1) * (PDI // 2)))
                    half, hb = load_piece(s_Wdt, mc * 2 + kh, "s_Wdt_hbm")
                    for i, kc in enumerate(kcs):
                        pe(lambda e, i=i, kc=kc, half=half, mb=mb: e.matmul(
                            psum[:, mb:mb + COLS], piece_tile(half, i),
                            AP(u, kc * COLS, [[1, 128], [1, COLS]]),
                            start=(kc == 0), stop=(kc == PDI - 1)),
                           [hb, UB(kc, 0), UB(kc, 1)], [mn],
                           inc=(kc == PDI - 1))
                act(AP(e_ring, 0, [[1, 128], [1, COLS]]),
                    psum[:, mb:mb + COLS], AF.Exp, [mn, "bcol_dt"],
                    ["e_ringFR"] + (["ppw2"] if p == 0 and mc == 0
                                    and NSW > 2 else []),
                    bias=bcol_dt[:, mc:mc + 1])
                act(AP(dtw, mc * COLS, [[1, 128], [1, COLS]]),
                    AP(e_ring, 0, [[1, 128], [1, COLS]]), AF.Ln,
                    ["e_ringFR", "c_one"], [BUFN(dtw, mc)],
                    bias=c_one[:, 0:1])
                if mc % 2 == 1:
                    yield
            tr.section = "FRwz@%d" % p
            if p > 0:
                yield ("COND", ("wout", p - 1), 1)
            for mc in range(PDI):
                half, hb = load_piece(s_Win, PDI + mc, "s_Win_hbm")
                mb, mn = next_mmbank()
                for kc in range(PD):
                    pe(lambda e, kc=kc, half=half, mb=mb: e.matmul(
                        psum[:, mb:mb + COLS], piece_tile(half, kc),
                        AP(scr16b, kc * COLS, [[1, 128], [1, COLS]]),
                        start=(kc == 0), stop=(kc == PD - 1)),
                       [hb, "scr16b"], [mn], inc=(kc == PD - 1))
                act(AP(szw, mc * COLS, [[1, 128], [1, COLS]]),
                    psum[:, mb:mb + COLS], AF.Silu, [mn, "bcol_in"],
                    [BUFN(szw, mc)],
                    bias=bcol_in[:, PDI + mc:PDI + mc + 1])
                if mc % 4 == 3:
                    yield
            dbg_tap("d_u", u, 0, PDI * COLS, UB(PDI - 1, 1), p)
            dbg_tap("d_sz", szw, 0, PDI * COLS, BUFN(szw, PDI - 1), p)
            dbg_tap("d_dt", dtw, 0, PDI * COLS, BUFN(dtw, PDI - 1), p)

        def gen_S(p):
            grp = p % 2
            t0 = p * W
            dtb, szb = DTBUF(p), SZBUF(p)
            sqp = COLS
            tr.section = "Sbc@%d" % p
            for s in range(DS):
                for b in range(NB):
                    dma("QG", AP(Bfull, s * COLS + b * W, [[1, 128], [1, W]]),
                        bass.AP(sq_ring, s * sqp + b * W,
                                [[sqp, 1], [0, 128], [1, W]]),
                        ["sq_ring"], ["Bfull"])
                    dma("QG", AP(Cfull, s * COLS + b * W, [[1, 128], [1, W]]),
                        bass.AP(sq_ring, (DS + s) * sqp + b * W,
                                [[sqp, 1], [0, 128], [1, W]]),
                        ["sq_ring"], ["Cfull"])
            yield
            NSH = ceil_div(DS, SH)
            dA2first = [True]
            for dblk in range(PDI):
                tr.section = "S.wt@%d" % p
                dve(lambda e, dblk=dblk: e.tensor_tensor(
                    wtile[:], AP(dtb, dblk * COLS, [[1, 128], [1, COLS]]),
                    AP(u, dblk * COLS, [[1, 128], [1, COLS]]), ALU.mult),
                    [BUFN(dtb, dblk), UB(dblk, 0), UB(dblk, 1)], ["wtile"],
                    inc=False)
                for sh in range(NSH):
                    s0 = sh * SH
                    ns = min(SH, DS - s0)
                    par = ((dblk * NSH) + sh) % 2
                    dan = "dA%d" % par
                    dbuf, dbase = (dA, 0) if par == 0 else (scr16a, 0)
                    extraw = []
                    if par == 1 and dA2first[0]:
                        dA2first[0] = False
                        extraw = ["scr16a"]
                    tr.section = "S.dA@%d" % p
                    for sl in range(ns):
                        s = s0 + sl
                        act(AP(dbuf, dbase + sl * COLS,
                               [[1, 128], [1, COLS]]),
                            AP(dtb, dblk * COLS, [[1, 128], [1, COLS]]),
                            AF.Exp, [BUFN(dtb, dblk), "Acols"],
                            [dan] + extraw,
                            scale=Acols[:, dblk * DS + s:dblk * DS + s + 1],
                            inc=(sl == ns - 1))
                        extraw = []
                    for b in range(NB):
                        tr.section = "S.dbu@%d" % p
                        dve(lambda e, b=b, s0=s0, ns=ns: e.tensor_tensor(
                            AP(Parena, 0, [[1, 128], [W, ns], [1, W]]),
                            bass.AP(wtile, b * W,
                                    [[COLS, 128], [0, ns], [1, W]]),
                            AP(Bfull, s0 * COLS + b * W,
                               [[1, 128], [COLS, ns], [1, W]]),
                            ALU.mult),
                            ["wtile", "Bfull"], ["Parena"], inc=False)
                        tr.section = "S.tts@%d" % p
                        for sl in range(ns):
                            s = s0 + sl
                            ccol = b * PDI * DS + dblk * DS + s
                            dve(lambda e, sl=sl, b=b, ccol=ccol,
                                dbase=dbase, dbuf=dbuf:
                                e.tensor_tensor_scan(
                                    AP(Harena, sl * W, [[1, 128], [1, W]]),
                                    AP(dbuf, dbase + sl * COLS + b * W,
                                       [[1, 128], [1, W]]),
                                    AP(Parena, sl * W, [[1, 128], [1, W]]),
                                    carry[:, ccol:ccol + 1], ALU.mult,
                                    ALU.add),
                                [dan, "Parena", nmf("cr", b, dblk, sh)],
                                ["Harena"], inc=False)
                        tr.section = "S.cm@%d" % p
                        dve(lambda e, s0=s0, ns=ns, b=b: e.tensor_tensor(
                            AP(Parena, 0, [[1, 128], [1, ns * W]]),
                            AP(Harena, 0, [[1, 128], [1, ns * W]]),
                            AP(Cfull, s0 * COLS + b * W,
                               [[1, 128], [COLS, ns], [1, W]]),
                            ALU.mult),
                            ["Harena", "Cfull"], ["Parena"], inc=False)
                        cbase = b * PDI * DS + dblk * DS + s0
                        tr.section = "S.carry@%d" % p
                        dve(lambda e, cbase=cbase, ns=ns: e.tensor_copy(
                            carry[:, cbase:cbase + ns],
                            AP(Harena, W - 1, [[1, 128], [W, ns]])),
                            ["Harena"], [nmf("cr", b, dblk, sh)],
                            inc=False)
                        nsl = ns
                        tr.section = "S.tree@%d" % p
                        srcbuf, soff, srcname = Parena, 0, "Parena"
                        use_h = 0
                        while nsl > 1:
                            hn = nsl // 2
                            doff = use_h
                            use_h += hn * W
                            dve(lambda e, hn=hn, srcbuf=srcbuf, soff=soff,
                                doff=doff: e.tensor_tensor(
                                    AP(Harena, doff,
                                       [[1, 128], [W, hn], [1, W]]),
                                    AP(srcbuf, soff,
                                       [[1, 128], [2 * W, hn], [1, W]]),
                                    AP(srcbuf, soff + W,
                                       [[1, 128], [2 * W, hn], [1, W]]),
                                    ALU.add), [srcname], ["Harena"],
                                inc=False)
                            srcbuf, soff, srcname = Harena, doff, "Harena"
                            nsl = hn
                        if sh == 0:
                            dve(lambda e, b=b, srcbuf=srcbuf, soff=soff:
                                e.tensor_copy(
                                    AP(ypart, b * W, [[1, 128], [1, W]]),
                                    AP(srcbuf, soff, [[1, 128], [1, W]])),
                                [srcname], ["ypart"], inc=False)
                        else:
                            dve(lambda e, b=b, srcbuf=srcbuf, soff=soff:
                                e.tensor_tensor(
                                    AP(ypart, b * W, [[1, 128], [1, W]]),
                                    AP(ypart, b * W, [[1, 128], [1, W]]),
                                    AP(srcbuf, soff, [[1, 128], [1, W]]),
                                    ALU.add),
                                [srcname, "ypart"], ["ypart"], inc=False)
                tr.section = "S.stt@%d" % p
                for b in range(NB):
                    dve(lambda e, dblk=dblk, b=b: e.scalar_tensor_tensor(
                        AP(wtile, b * W, [[1, 128], [1, W]]),
                        AP(u, dblk * COLS + b * W, [[1, 128], [1, W]]),
                        Dcols[:, dblk:dblk + 1],
                        AP(ypart, b * W, [[1, 128], [1, W]]),
                        ALU.mult, ALU.add),
                        ["ypart", UB(dblk, b), "Dcols"], ["wtile"],
                        inc=False)
                    dve(lambda e, dblk=dblk, b=b: e.tensor_tensor(
                        AP(dtb, dblk * COLS + b * W, [[1, 128], [1, W]]),
                        AP(wtile, b * W, [[1, 128], [1, W]]),
                        AP(szb, dblk * COLS + b * W, [[1, 128], [1, W]]),
                        ALU.mult),
                        ["wtile", BUFN(szb, dblk), UB(dblk, b)],
                        [BUFN(dtb, dblk)], inc=True)
                state[("scan", p)] = dblk + 1
                yield
            dbg_tap("d_yg", dtb, 0, PDI * COLS, BUFN(dtb, PDI - 1), p)
            yield "WAIT_F3"
            tr.section = "Sxrel@%d" % p
            for b in range(NB):
                for ts in range(NTS):
                    rws = min(128, W - ts * 128)
                    dma("QS", AP(stage, 0, [[1, rws], [1, D]]),
                        x_in.ap()[b, t0 + ts * 128:t0 + ts * 128 + rws, :],
                        ["stage"], ["stage"])
                    for g0 in range(0, PD, 4):
                        gn = min(4, PD - g0)
                        mbt, mnt = next_mmbank()
                        for q in range(gn):
                            dpb = g0 + q
                            pe(lambda e, dpb=dpb, rws=rws, mbt=mbt, q=q,
                               gn=gn: e.matmul(
                                   psum[0:128,
                                        mbt + q * rws:mbt + (q + 1) * rws],
                                   AP(stage, dpb * 128, [[1, rws], [1, 128]]),
                                   idn[0:rws, 0:rws],
                                   start=(q == 0), stop=(q == gn - 1)),
                               ["stage", "idn"], [mnt], inc=True)
                        act(AP(x_res, g0 * COLS + b * W + ts * 128,
                               [[1, 128], [COLS, gn], [1, rws]]),
                            psum[0:128, mbt:mbt + gn * rws], AF.Copy, [mnt],
                            ["x_res"])
            yield
            tr.section = "Swout@%d" % p
            for mc in range(PD):
                mb, mn = next_mmbank()
                for kh in range(2):
                    kcs = list(range(kh * (PDI // 2), (kh + 1) * (PDI // 2)))
                    half, hb = load_piece(s_Wout, mc * 2 + kh, "s_Wout_hbm")
                    for i, kc in enumerate(kcs):
                        pe(lambda e, i=i, kc=kc, half=half, mb=mb:
                           e.matmul(
                               psum[:, mb:mb + COLS], piece_tile(half, i),
                               AP(dtb, kc * COLS, [[1, 128], [1, COLS]]),
                               start=(kc == 0), stop=(kc == PDI - 1)),
                           [hb, BUFN(dtb, kc)], [mn],
                           inc=(kc == PDI - 1))
                act(AP(scr16a, mc * COLS, [[1, 128], [1, COLS]]),
                    psum[:, mb:mb + COLS], AF.Identity, [mn, "bcol_out"],
                    ["scr16a"], bias=bcol_out[:, mc:mc + 1])
                if mc % 2 == 1:
                    yield
            state[("wout", p)] = 1
            tr.section = "Slnmo@%d" % p
            st = stats("scr16a", scr16a, 0, PD, grp)
            ln_rows(grp, D, st)
            ln_apply_ip(grp, scr16a, "scr16a", PD)
            dbg_tap("d_mo", scr16a, 0, PD * COLS, "scr16a", p)
            tr.section = "Sres1@%d" % p
            dve(lambda e: e.tensor_tensor(x_res[:], x_res[:], scr16a[:],
                                         ALU.add),
               ["x_res", "scr16a"], ["x_res"])
            dbg_tap("d_x1", x_res, 0, PD * COLS, "x_res", p)
            tr.section = "Sln2@%d" % p
            act(scr16a[:], x_res[:], AF.Copy, ["x_res"], ["scr16a"])
            st = stats("scr16a", scr16a, 0, PD, grp)
            ln_rows(grp, D, st)
            ln_apply_ip(grp, scr16a, "scr16a", PD)
            yield
            if p > 0:
                yield ("COND", ("loop1", p - 1), 1)
            tr.section = "Sq1@%d" % p
            dve(lambda e: e.tensor_tensor(
                colmax[:, 0:COLS], scr16a[:, 0:COLS], scr16a[:, COLS:2 * COLS],
                ALU.max), ["scr16a"],
                ["colmaxS"] + (["ppw4"] if p == 0 and NSW > 4 else []))
            dve(lambda e: e.tensor_tensor(
                colmin[:, 0:COLS], scr16a[:, 0:COLS], scr16a[:, COLS:2 * COLS],
                ALU.min), ["scr16a"], ["colminS"])
            for i in range(2, PD):
                dve(lambda e, i=i: e.tensor_tensor(
                    colmax[:, 0:COLS], colmax[:, 0:COLS],
                    AP(scr16a, i * COLS, [[1, 128], [1, COLS]]), ALU.max),
                    ["scr16a", "colmaxS"], ["colmaxS"])
                dve(lambda e, i=i: e.tensor_tensor(
                    colmin[:, 0:COLS], colmin[:, 0:COLS],
                    AP(scr16a, i * COLS, [[1, 128], [1, COLS]]), ALU.min),
                    ["scr16a", "colminS"], ["colminS"])
            colmax_to_m1(grp, 0, COLS, ["colmaxS", "colminS"])
            quant_rows(grp, COLS)
            dve(lambda e: e.tensor_scalar(row(grp, "S1INV"), row(grp, "TMP"),
                                          1.0 / 127.0, None, ALU.mult),
                ["rows"], ["rows"])
            bcast_row(grp, "S1")
            bcast_row(grp, "S1INV")
            for i in range(PD):
                dve(lambda e, i=i: e.tensor_tensor(
                    AP(e_ring, COLS, [[1, 128], [1, COLS]]),
                    AP(scr16a, i * COLS, [[1, 128], [1, COLS]]),
                    bct_t(grp, "S1"), ALU.mult),
                    ["scr16a", bct_nm(grp, "S1")],
                    ["e_ring"] + (["ppw3"] if p == 0 and i == 0
                                  and NSW > 3 else []))
                dve(lambda e: e.tensor_scalar(
                    AP(i32_ring, 0, [[1, 128], [1, COLS]]),
                    AP(e_ring, COLS, [[1, 128], [1, COLS]]), MAGIC, MAGIC,
                    ALU.add, ALU.subtract), ["e_ring"], ["i32_ring"])
                dve(lambda e, i=i: e.tensor_tensor(
                    AP(xqs, i * COLS, [[1, 128], [1, COLS]]),
                    AP(i32_ring, 0, [[1, 128], [1, COLS]]),
                    bct_t(grp, "S1INV"), ALU.mult),
                    ["i32_ring", bct_nm(grp, "S1INV")], [XQ(i)])
                if i % 4 == 3:
                    yield
            dbg_tap("d_xqs", xqs, 0, PD * COLS, XQ(PD - 1), p)

        def gen_F3(p):
            grp = p % 2
            t0 = p * W
            for b in range(NB):
                tr.section = "F3q2m@%d" % p
                dve(lambda e: e.memset(colmax[:, COLS:COLS + W], -1e30), [],
                   ["colmaxF"])
                dve(lambda e: e.memset(
                    AP(e_ring, 2 * COLS, [[1, 128], [1, W]]), -1e30), [],
                   ["e_ringF"])
                dve(lambda e: e.memset(vq_ring[:, 0:W], 1e30), [],
                   ["vq_ring"])
                dve(lambda e: e.memset(colmin[:, COLS:COLS + W], 1e30),
                   [], ["colminF"])
                def pair_tail(ginfo, uinfo):
                    tr.section = "F3silu@%d" % p
                    for oc, mbg, mng in ginfo:
                        gr = (oc % 2) * W
                        act(g_ring[:, gr:gr + W], psum[:, mbg:mbg + W],
                            AF.Silu, [mng, "mw_col"], ["g%d" % (oc % 2)],
                            scale=mw_col[:, 0:1])
                    tr.section = "F3vmul@%d" % p
                    for oc, mbu, mnu in uinfo:
                        gr = (oc % 2) * W
                        vsl = AP(vbuf, oc * W, [[1, 128], [1, W]])
                        dve(lambda e, vsl=vsl, gr=gr, mbu=mbu:
                            e.tensor_tensor(
                                vsl, g_ring[:, gr:gr + W],
                                psum[:, mbu:mbu + W],
                                ALU.mult), ["g%d" % (oc % 2), mnu], [VB(oc)])
                    for oc, mbu, mnu in uinfo:
                        vsl = AP(vbuf, oc * W, [[1, 128], [1, W]])
                        if oc % 2 == 0:
                            mxap = colmax[:, COLS:COLS + W]
                            mxnm = "colmaxF"
                            mnap = colmin[:, COLS:COLS + W]
                            mnnm = "colminF"
                        else:
                            mxap = AP(e_ring, 2 * COLS, [[1, 128], [1, W]])
                            mxnm = "e_ringF"
                            mnap = vq_ring[:, 0:W]
                            mnnm = "vq_ring"
                        dve(lambda e, vsl=vsl, mxap=mxap:
                            e.tensor_tensor(
                                mxap, mxap, vsl, ALU.max), [VB(oc), mxnm],
                            [mxnm])
                        dve(lambda e, vsl=vsl, mnap=mnap:
                            e.tensor_tensor(
                                mnap, mnap, vsl, ALU.min), [VB(oc), mnnm],
                            [mnnm])

                pend = None
                for op in range(0, PFF, 2):
                    ocs = [oc for oc in (op, op + 1) if oc < PFF]
                    tr.section = "F3gmm@%d" % p
                    ginfo = []
                    for oc in ocs:
                        halfg, hbg = load_piece_f(s_wg, oc, "s_wg_hbm")
                        mbg, mng = next_mmbank()
                        for kc in range(PD):
                            pe(lambda e, kc=kc, halfg=halfg, b=b, mbg=mbg:
                               e.matmul(
                                   psum[:, mbg:mbg + W],
                                   piece_tile_f(halfg, kc),
                                   AP(xqs, kc * COLS + b * W,
                                      [[1, 128], [1, W]]),
                                   start=(kc == 0), stop=(kc == PD - 1)),
                               [hbg, XQ(kc)], [mng], inc=(kc == PD - 1))
                        ginfo.append((oc, mbg, mng))
                    tr.section = "F3umm@%d" % p
                    uinfo = []
                    for oc in ocs:
                        halfu, hbu = load_piece_f(s_wu, oc, "s_wu_hbm")
                        mbu, mnu = next_mmbank()
                        for kc in range(PD):
                            pe(lambda e, kc=kc, halfu=halfu, b=b, mbu=mbu:
                               e.matmul(
                                   psum[:, mbu:mbu + W],
                                   piece_tile_f(halfu, kc),
                                   AP(xqs, kc * COLS + b * W,
                                      [[1, 128], [1, W]]),
                                   start=(kc == 0), stop=(kc == PD - 1)),
                               [hbu, XQ(kc)], [mnu], inc=(kc == PD - 1))
                        uinfo.append((oc, mbu, mnu))
                    if pend is not None:
                        pair_tail(*pend)
                    pend = (ginfo, uinfo)
                    yield
                pair_tail(*pend)
                if b == NB - 1:
                    state[("loop1", p)] = 1
                if b == 0:
                    dbg_tap("d_v", vbuf, 0, PFF * W, VB(PFF - 1), p)
                tr.section = "F3q2@%d" % p
                dve(lambda e: e.tensor_tensor(
                    colmax[:, COLS:COLS + W], colmax[:, COLS:COLS + W],
                    AP(e_ring, 2 * COLS, [[1, 128], [1, W]]), ALU.max),
                    ["colmaxF", "e_ringF"], ["colmaxF"])
                dve(lambda e: e.tensor_tensor(
                    colmin[:, COLS:COLS + W], colmin[:, COLS:COLS + W],
                    vq_ring[:, 0:W], ALU.min),
                    ["colminF", "vq_ring"], ["colminF"])
                colmax_to_m1(grp, COLS, W, ["colmaxF", "colminF"])
                quant_rows(grp, W)
                bcast_row(grp, "S1", W)
                yield
                mpr = max(1, 512 // W)
                tr.section = "F3dn@%d" % p
                dn_open[0] = True
                for op2 in range(0, PFF, 2):
                    ocs2 = [oc for oc in (op2, op2 + 1) if oc < PFF]
                    lds = []
                    for oc in ocs2:
                        half, hb = load_piece_f(s_wd, oc, "s_wd_hbm")
                        lds.append((oc, half, hb))
                    for oc in ocs2:
                        dve(lambda e, oc=oc: e.tensor_tensor(
                            AP(colmax, (oc % 2) * W, [[1, 128], [1, W]]),
                            AP(vbuf, oc * W, [[1, 128], [1, W]]),
                            bct_t(grp, "S1", W), ALU.mult),
                           [VB(oc), bct_nm(grp, "S1")], ["colmaxS"])
                    for oc in ocs2:
                        dve(lambda e, oc=oc: e.tensor_scalar(
                            AP(i32_ring, (oc % 2) * W, [[1, 128], [1, W]]),
                            AP(colmax, (oc % 2) * W, [[1, 128], [1, W]]),
                            MAGIC, MAGIC, ALU.add, ALU.subtract),
                           ["colmaxS"], ["i32_ring"])
                    for oc in ocs2:
                        dve(lambda e, oc=oc: e.tensor_copy(
                            vq_ring[:, (oc % 2) * W:(oc % 2) * W + W],
                            AP(i32_ring, (oc % 2) * W, [[1, 128], [1, W]])),
                           ["i32_ring"], ["vq_ring"])
                    for oc, half, hb in lds:
                        vr = (oc % 2) * W
                        for mc in range(PD):
                            pe(lambda e, mc=mc, half=half, vr=vr, oc=oc,
                               mpr=mpr: e.matmul(
                                   psDN(mc, W), piece_tile_f(half, mc),
                                   vq_ring[:, vr:vr + W],
                                   start=(oc == 0 and mc % mpr == 0),
                                   stop=(oc == PFF - 1 and
                                         (mc % mpr == mpr - 1
                                          or mc == PD - 1))),
                               [hb, "vq_ring"], ["mmb%d" % (4 + mc // mpr)],
                               inc=(oc == PFF - 1 and mc == PD - 1))
                    yield
                tr.section = "F3dnev@%d" % p
                for k0 in range(0, PD, mpr):
                    kn = min(mpr, PD - k0)
                    act(AP(scr16c, k0 * COLS + b * W,
                           [[1, 128], [COLS, kn], [1, W]]),
                        psum[:, 2048 + k0 * W:2048 + (k0 + kn) * W], AF.Copy,
                        ["mmb%d" % (4 + k0 // mpr)],
                        ["scr16c"] + (["ppt%d" % i for i in range(NST)]
                                      if p == 0 and b == 0 and k0 == 0
                                      else []))
                dn_open[0] = False
                yield
            dbg_tap("d_ffo", scr16c, 0, PD * COLS, "scr16c", p)
            tr.section = "F3ln3@%d" % p
            st = stats("scr16c", scr16c, 0, PD, grp)
            ln_rows(grp, D, st)
            ln_apply_ip(grp, scr16c, "scr16c", PD)
            yield
            tr.section = "F3res2@%d" % p
            dve(lambda e: e.tensor_tensor(x_res[:], x_res[:], scr16c[:],
                                         ALU.add),
               ["x_res", "scr16c"], ["x_res"])
            tr.section = "F3go@%d" % p
            for b in range(NB):
                for ts in range(NTS):
                    rws = min(128, W - ts * 128)
                    for g0 in range(0, PD, 4):
                        gn = min(4, PD - g0)
                        mbt, mnt = next_mmbank()
                        for q in range(gn):
                            dpb = g0 + q
                            act(AP(e_ring, 2 * COLS,
                                   [[1, 128], [1, rws]]),
                                AP(x_res, dpb * COLS + b * W + ts * 128,
                                   [[1, 128], [1, rws]]),
                                AF.Copy, ["x_res"], ["e_ringF"])
                            pe(lambda e, q=q, rws=rws, mbt=mbt, gn=gn:
                               e.matmul(
                                   psum[0:rws,
                                        mbt + q * 128:mbt + (q + 1) * 128],
                                   AP(e_ring, 2 * COLS,
                                      [[1, 128], [1, rws]]),
                                   idn[:], start=(q == 0),
                                   stop=(q == gn - 1)),
                               ["e_ringF", "idn"], [mnt], inc=True)
                        act(AP(stage, g0 * 128, [[1, rws], [1, gn * 128]]),
                            psum[0:rws, mbt:mbt + gn * 128], AF.Copy, [mnt],
                            ["stage"] + (["ppw0", "ppw1"] if p == 0 else []))
                    dma("QG",
                        x_out.ap()[b, t0 + ts * 128:t0 + ts * 128 + rws, :],
                        AP(stage, 0, [[1, rws], [1, D]]),
                        ["stage"], ["xout_hbm"])
                    yield

        state = {}

        def drive2(f3, s, fr):
            gens = [g for g in (f3, s, fr) if g is not None]
            f3_alive = f3 is not None
            blocked = []
            while gens or blocked:
                progressed = False
                for item in list(blocked):
                    g, kind, key, val = item
                    ok = (kind == "F3" and not f3_alive) or \
                        (kind == "COND" and state.get(key, 0) >= val)
                    if ok:
                        blocked.remove(item)
                        gens.append(g)
                        progressed = True
                for g in list(gens):
                    try:
                        r = next(g)
                        progressed = True
                        if r == "WAIT_F3" and f3_alive:
                            gens.remove(g)
                            blocked.append((g, "F3", None, 0))
                        elif isinstance(r, tuple) and r[0] == "COND":
                            if state.get(r[1], 0) < r[2]:
                                gens.remove(g)
                                blocked.append((g, "COND", r[1], r[2]))
                    except StopIteration:
                        gens.remove(g)
                        progressed = True
                        if g is f3:
                            f3_alive = False
                if not progressed and (gens or blocked):
                    raise RuntimeError("scheduler deadlock: %r" %
                                       [b[1:] for b in blocked])

        drive2(None, None, gen_FR(0))
        for p in range(NPASS):
            drive2(gen_F3(p - 1) if p > 0 else None,
                   gen_S(p),
                   gen_FR(p + 1) if p + 1 < NPASS else None)
        drive2(gen_F3(NPASS - 1), None, None)

        qg_final = [("w", f"QG{i}", tr.cnt[f"QG{i}"]) for i in range(NSLOT)
                    if tr.cnt[f"QG{i}"] > 0]
        qs_final = [("w", f"QS{i}", tr.cnt[f"QS{i}"]) for i in range(NSLOT)
                    if tr.cnt[f"QS{i}"] > 0]
        tr.records.append(("QG", qg_final, None, "QG0", 0))
        tr.records.append(("QS", qs_final, None, "QS0", 0))
        tr.labels.append("final")
        tr.labels.append("final")

        sems = {}
        for k in tr.sems:
            sems[k] = es.enter_context(nc.semaphore("sem_" + k))
        blk = es.enter_context(nc.Block())
        ENGSEL = {"PE": "tensor", "ACT": "scalar", "DVE": "vector",
                  "QS": "sync", "QG": "gpsimd"}
        by_eng = {k: [] for k in ENGSEL}
        for rec in tr.records:
            by_eng["QG" if rec[0] == "GP" else rec[0]].append(rec)

        def run_stream(eng_obj, recs):
            waited = {}
            for eng, pre, emitfn, sem_tag, step in recs:
                for a in pre:
                    if a[0] == "d":
                        eng_obj.drain()
                    else:
                        _, sm, pc = a
                        val = pc.val if isinstance(pc, Pending) else pc
                        if val is None:
                            raise RuntimeError(
                                f"unresolved dep {sm} for {eng}")
                        if waited.get(sm, -1) < val:
                            eng_obj.wait_ge(sems[sm], val)
                            waited[sm] = val
                if emitfn == "BUMP":
                    eng_obj.sem_inc(sems[sem_tag], 1)
                    continue
                inst = emitfn(eng_obj) if emitfn is not None else None
                if step and inst is not None:
                    inst.then_inc(sems[sem_tag], step)

        @blk.tensor
        def _(e):
            run_stream(e, by_eng["PE"])

        @blk.scalar
        def _(e):
            run_stream(e, by_eng["ACT"])

        @blk.vector
        def _(e):
            run_stream(e, by_eng["DVE"])

        @blk.sync
        def _(e):
            run_stream(e, by_eng["QS"])

        @blk.gpsimd
        def _(e):
            run_stream(e, by_eng["QG"])

    dbg["_labels"] = {
        eng: [lab for rec, lab in zip(tr.records, tr.labels)
              if rec[0] == eng and rec[2] is not None and rec[2] != "BUMP"]
        for eng in ("PE", "ACT", "DVE")}
    return nc, dbg




_CACHE = {}


def _get_nc():
    if "nc" not in _CACHE:
        _CACHE["nc"] = build_program(full_cfg())[0]
    return _CACHE["nc"]


def kernel(L0_attn_x, L0_attn_W_in, L0_attn_b_in, L0_attn_W_xproj,
           L0_attn_b_xproj, L0_attn_W_dt, L0_attn_b_dt, L0_attn_W_out,
           L0_attn_b_out, L0_attn_A_log, L0_attn_D_param, L0_ffn_w_gate,
           L0_ffn_w_up, L0_ffn_w_down):
    import ml_dtypes
    from concourse.bass_utils import run_bass_kernel_spmd
    n = 8
    x = np.ascontiguousarray(np.asarray(L0_attn_x, dtype=np.float32))
    shards = [np.ascontiguousarray(x[2 * i:2 * i + 2]) for i in range(n)]
    common = dict(
        W_in=np.ascontiguousarray(np.asarray(L0_attn_W_in, np.float32)),
        b_in=np.ascontiguousarray(np.asarray(L0_attn_b_in, np.float32)),
        W_xp=np.ascontiguousarray(np.asarray(L0_attn_W_xproj, np.float32)),
        b_xp=np.ascontiguousarray(np.asarray(L0_attn_b_xproj, np.float32)),
        W_dt=np.ascontiguousarray(np.asarray(L0_attn_W_dt, np.float32)),
        b_dt=np.ascontiguousarray(np.asarray(L0_attn_b_dt, np.float32)),
        W_out=np.ascontiguousarray(np.asarray(L0_attn_W_out, np.float32)),
        b_out=np.ascontiguousarray(np.asarray(L0_attn_b_out, np.float32)),
        A_log=np.ascontiguousarray(np.asarray(L0_attn_A_log, np.float32)),
        D_par=np.ascontiguousarray(np.asarray(L0_attn_D_param, np.float32)),
        w_g=np.ascontiguousarray(np.asarray(L0_ffn_w_gate, np.float32)),
        w_u=np.ascontiguousarray(np.asarray(L0_ffn_w_up, np.float32)),
        w_d=np.ascontiguousarray(np.asarray(L0_ffn_w_down, np.float32)),
        ident=np.eye(128, dtype=np.float32),
        identb=np.eye(128, dtype=ml_dtypes.bfloat16),
    )
    in_maps = [dict(common, x_in=shards[i]) for i in range(n)]
    nc = _get_nc()
    res = run_bass_kernel_spmd(nc, in_maps, core_ids=list(range(n)))
    out = np.concatenate([r["x_out"] for r in res.results], axis=0)
    return np.ascontiguousarray(out.astype(np.float32))
